# version 12
# baseline (speedup 1.0000x reference)
import numpy as np
import ml_dtypes
import concourse.bass as bass
import concourse.mybir as mybir
from concourse.bass_utils import run_bass_kernel_spmd
from concourse.aot_env import allow_ici_all_to_all
from contextlib import ExitStack

F32 = mybir.dt.float32
BF16 = mybir.dt.bfloat16
F8 = mybir.dt.float8e4
I32 = mybir.dt.int32
AF = mybir.ActivationFunctionType
ALU = mybir.AluOpType

NCORE = 8
D = 1024
DL = 128
BB = 4
L = 4096
TOKS = BB * L
TL = 2048
NT = TL // 128
T = 128
CH = 32
U = BB * CH
NM = 32
NE = 129
GEN = NE * NM
GW = 4160

PI = float(np.pi)
TWO_PI = 2.0 * PI
INV2PI = 1.0 / TWO_PI

NGRP = 8
GC = 16
NB = 32


def build_program():
    nc = bass.Bass("TRN2", target_bir_lowering=False)

    xs = nc.dram_tensor("xs", [TL, D], F32, kind="ExternalInput")
    ldt = nc.dram_tensor("ldt", [DL, 1], F32, kind="ExternalInput")
    lar = nc.dram_tensor("lar", [DL, NM], F32, kind="ExternalInput")
    aim = nc.dram_tensor("aim", [DL, NM], F32, kind="ExternalInput")
    cre = nc.dram_tensor("cre", [DL, NM], F32, kind="ExternalInput")
    cim = nc.dram_tensor("cim", [DL, NM], F32, kind="ExternalInput")
    WTd = nc.dram_tensor("WT", [D, D], F32, kind="ExternalInput")
    WFTd = nc.dram_tensor("WFT", [D, D], F32, kind="ExternalInput")
    battn_r = nc.dram_tensor("battn_r", [1, D], F32, kind="ExternalInput")
    bffn_r = nc.dram_tensor("bffn_r", [1, D], F32, kind="ExternalInput")
    isc = nc.dram_tensor("isc", [DL, 1], F32, kind="ExternalInput")
    wsc = nc.dram_tensor("wsc", [DL, 1], F32, kind="ExternalInput")
    iotaE = nc.dram_tensor("iotaE", [DL, NE], F32, kind="ExternalInput")
    jmat = nc.dram_tensor("jmat", [128, 128], BF16, kind="ExternalInput")
    ones1 = nc.dram_tensor("ones1", [1, 128], BF16, kind="ExternalInput")
    out_t = nc.dram_tensor("out", [TL, D], F32, kind="ExternalOutput")

    a1_in = nc.dram_tensor("a1_in", [NCORE, TL, DL], BF16)
    a1_out = nc.dram_tensor("a1_out", [TOKS, DL], BF16)
    y_scr = nc.dram_tensor("y_scr", [DL, TOKS], BF16)
    a2_in = nc.dram_tensor("a2_in", [NCORE, DL, TL], BF16)
    a2_out = nc.dram_tensor("a2_out", [D, TL], BF16)
    xq_scr = nc.dram_tensor("xq_scr", [TL, D], BF16)
    g_scr = nc.dram_tensor("g_scr", [TL, D], F32)
    kpadR = nc.dram_tensor("kpadR", [DL, 256], BF16)
    waT = nc.dram_tensor("waT", [T, DL, 64], BF16)
    eT1 = nc.dram_tensor("eT1", [DL, T, 128], BF16)
    rTre = nc.dram_tensor("rTre", [128, DL], F32)
    rTim = nc.dram_tensor("rTim", [128, DL], F32)
    uReD = nc.dram_tensor("uReD", [128, DL, CH], F32)
    uImD = nc.dram_tensor("uImD", [128, DL, CH], F32)
    s_scr = nc.dram_tensor("s_scr", [DL, 64, U], BF16)

    ctx = ExitStack()
    ST = lambda *a, **k: ctx.enter_context(nc.sbuf_tensor(*a, **k))
    PT = lambda *a: ctx.enter_context(nc.psum_tensor(*a))

    c_iota = ST("c_iota", [DL, NE], F32)
    c_j = ST("c_j", [128, 128], BF16)
    c_ones1 = ST("c_ones1", [1, 128], BF16)
    p_ldt = ST("p_ldt", [DL, 1], F32)
    p_lar = ST("p_lar", [DL, NM], F32)
    p_aim = ST("p_aim", [DL, NM], F32)
    p_cre = ST("p_cre", [DL, NM], F32)
    p_cim = ST("p_cim", [DL, NM], F32)
    p_isc = ST("p_isc", [DL, 1], F32)
    p_wsc = ST("p_wsc", [DL, 1], F32)
    v_invs = ST("v_invs", [DL, 1], F32)
    v_s2 = ST("v_s2", [DL, 1], F32)
    v_rs2 = ST("v_rs2", [DL, 1], F32)
    d_dt = ST("d_dt", [DL, 1], F32)
    d_elar = ST("d_elar", [DL, NM], F32)
    d_lre = ST("d_lre", [DL, NM], F32)
    d_lim = ST("d_lim", [DL, NM], F32)
    d_ia2 = ST("d_ia2", [DL, NM], F32)
    d_t1 = ST("d_t1", [DL, NM], F32)
    d_t2 = ST("d_t2", [DL, NM], F32)
    d_t3 = ST("d_t3", [DL, NM], F32)
    d_wre = ST("d_wre", [DL, NM], F32)
    d_wim = ST("d_wim", [DL, NM], F32)
    d_cfr = ST("d_cfr", [DL, NM], F32)
    d_cfi = ST("d_cfi", [DL, NM], F32)
    d_aimn = ST("d_aimn", [DL, NM], F32)
    t_K = ST("t_K", [DL, NE], F32)
    z128 = ST("z128", [DL, 128], BF16)

    gA = ST("gA", [128, GW], F32)
    gB = ST("gB", [128, GW], F32)
    gC = ST("gC", [128, GW], F32)
    gI = ST("gI", [128, GW], I32)
    m_are = ST("m_are", [128, DL], F32)
    m_aim = ST("m_aim", [128, DL], F32)
    w_t1 = ST("w_t1", [128, 128], F32)
    w_t2 = ST("w_t2", [128, 128], F32)
    w_t3 = ST("w_t3", [128, 128], F32)
    w_t4 = ST("w_t4", [128, 128], F32)

    x_tl = [gA[:, 0:1024], gA[:, 1024:2048], gA[:, 2048:3072]]
    xn_tl = [gB[:, 0:1024], gB[:, 1024:2048]]
    xq1_tl = [gB[:, 2048:3072], gB[:, 3072:4096]]
    gIf = gI[:, 0:GEN].bitcast(F32)

    def u_sb_ap(grp, off, dims):
        base = (grp % 2) * 64 * GW
        d2 = [list(d) for d in dims]
        return bass.AP(gI, base + off, d2).bitcast(F32)
    s_sb = [gA[:, 0:1024], gA[:, 1024:2048], gA[:, 2048:3072], gA[:, 3072:4096]]
    gg_sb = [gB[:, 0:1024], gB[:, 1024:2048], gB[:, 2048:3072], gB[:, 3072:4096]]
    xr_tl = [gC[:, 0:1024], gC[:, 1024:2048]]
    o_sb = [gC[:, 2048:3072], gC[:, 3072:4096]]
    gtmp = [gC[:, 2048:3072], gC[:, 3072:4096]]

    s_bn = ST("s_bn", [128, 12], F32)
    s_mv1 = ST("s_mv1", [128, 2], F32)
    s_mv2 = ST("s_mv2", [128, 2], F32)
    s_mean = ST("s_mean", [128, 1], F32)
    s_tmp = ST("s_tmp", [128, 1], F32)
    s_var = ST("s_var", [128, 1], F32)
    s_rstd = [ST("s_rstd0", [128, 1], F32), ST("s_rstd1", [128, 1], F32)]
    s_dmy = ST("s_dmy", [1, 1], F32)
    t_KB = ST("t_KB", [DL, NE], BF16)
    xnb_tl = [ST("xnb0", [128, D], BF16), ST("xnb1", [128, D], BF16)]
    powb = ST("powb", [128, 2 * GEN], BF16)
    powCb = powb[:, 0:GEN]
    powSb = powb[:, GEN:2 * GEN]
    waReB = powb[:, 0:GEN]
    waImB = powb[:, GEN:2 * GEN]
    scbRe = powb[:, 0:DL * CH]
    scbIm = powb[:, GEN:GEN + DL * CH]
    f8t = [ST("f8t0", [128, D], F8)] * 2
    xqbt = [powb[:, 0:D], powb[:, D:2 * D]]
    s_nmr = [ST("s_nmr0", [128, 1], F32), ST("s_nmr1", [128, 1], F32)]

    wft_b = ST("wft_b", [128, 8 * D], BF16)
    wt_b = wft_b
    bffn_row = ST("bffn_row", [1, D], BF16)
    bffn_sb = ST("bffn_sb", [1, D], BF16)
    battn_bb = ST("battn_bb", [1, D], BF16)
    xdqT = ST("xdqT", [128, 8 * TL], BF16, is_transpose_dest=True)

    x_all = ST("x_all", [128, U * 128], BF16)
    wa_b = [ST("wa_b0", [128, GC * 64], BF16), ST("wa_b1", [128, GC * 64], BF16)]
    hk_b = [ST("hk_b0", [128, GC * 128], BF16)] * 2
    e_b = [ST("e_b0", [128, GC * 128], BF16, is_transpose_dest=True)] * 2
    sm_b = [ST("sm_b0", [64, GC * U], BF16)] * 2
    yg_sb = [ST("yg_sb0", [128, 512], BF16), ST("yg_sb1", [128, 512], BF16)]
    yt_sb = [ST("yt_sb0", [128, 512], BF16), ST("yt_sb1", [128, 512], BF16)]
    sst_tl = [ST(f"sst{i}", [128, 8 * 128], BF16) for i in range(4)]

    psU = [PT("psU0", [64, 512], F32), PT("psU1", [64, 512], F32)]
    psUB = None
    psY = [PT("psY0", [128, 512], F32), PT("psY1", [128, 512], F32)]
    psT = [PT("psT0", [128, 512], BF16), PT("psT1", [128, 512], BF16)]
    psF = [PT("psF0", [128, 512], F32), PT("psF1", [128, 512], F32)]
    psUB = [psU[0], psU[1], psF[0], psF[1]]

    Q1 = ctx.enter_context(nc.semaphore("Q1"))
    XT = [ctx.enter_context(nc.semaphore(f"XT{i}")) for i in range(3)]
    SDX = [ctx.enter_context(nc.semaphore(f"SDX{i}")) for i in range(2)]
    SU = [ctx.enter_context(nc.semaphore(f"SU{i}")) for i in range(2)]
    GWS = [ctx.enter_context(nc.semaphore(f"GWS{i}")) for i in range(2)]
    SYP = [ctx.enter_context(nc.semaphore(f"SYP{i}")) for i in range(2)]
    XW = [ctx.enter_context(nc.semaphore(f"XW{i}")) for i in range(2)]
    GS = ctx.enter_context(nc.semaphore("GS"))
    QP = [ctx.enter_context(nc.semaphore(f"QP{i}")) for i in range(4)]
    QW = [ctx.enter_context(nc.semaphore(f"QW{i}")) for i in range(2)]
    GPD = ctx.enter_context(nc.semaphore("GPD"))
    GPC = ctx.enter_context(nc.semaphore("GPC"))
    VS = ctx.enter_context(nc.semaphore("VS"))
    SS = ctx.enter_context(nc.semaphore("SS"))
    VX = ctx.enter_context(nc.semaphore("VX"))
    SX = ctx.enter_context(nc.semaphore("SX"))
    TS = ctx.enter_context(nc.semaphore("TS"))

    M = {}
    q = 10
    M["q1_params"] = q * 16
    for i in range(NT):
        M[f"xt_{i}"] = 16 * (i // 3 + 1)
    q += 1; M["q1_zs"] = q * 16
    q += 8; M["q1_tr15"] = q * 16
    q += 16; M["q1_xall"] = q * 16
    for g in range(NGRP):
        M[f"qw_{g}"] = 16 * (g // 2 + 1)
    q += 2; M["q1_mult"] = q * 16
    q += 2; M["q1_scanrd"] = q * 16
    q += 4; M["q1_sscrw"] = q * 16
    for g in range(NGRP):
        q += 1; M[f"q1_hk{g}"] = q * 16
        q += 2; M[f"q1_e{g}"] = q * 16
        q += 1; M[f"q1_sm{g}"] = q * 16
    for i in range(NT):
        M[f"qp_{i}"] = 48 * (i // 4 + 1)
    M["q1_final"] = q * 16

    for i in range(NT):
        M[f"sdx_{i}"] = 16 * (i // 2 + 1)
        M[f"gw_{i}"] = 16 * (i // 2 + 1)
    for g in range(NGRP):
        M[f"su_{g}"] = 32 * (g // 2 + 1)
    for k in range(NB):
        M[f"sy_{k}"] = 16 * (k // 2 + 1)
    for i in range(NT):
        M[f"xw_{i}"] = 16 * (i // 2 + 1)
    g = 0
    g += 4; M["gs_eT"] = g * 16
    g += 2; M["gs_waTim"] = g * 16
    g += 2; M["gs_kpad"] = g * 16
    g += 8; M["gs_rT"] = g * 16


    p = 0
    p += 9; M["gpd_wft"] = p * 16
    p += 1; M["gpd_yshuf"] = p * 16
    p += 8; M["gpd_wt"] = p * 16
    p += 1; M["gpd_battn"] = p * 16
    for i in range(NT):
        p += 1; M[f"gpd_out{i}"] = p * 16
    M["gpd_final"] = p * 16

    M["gpc_cc1"] = 1
    M["gpc_cc2"] = 2
    for i in range(NT):
        M[f"gpc_comb{i}"] = 2 + 2 * (i + 1)

    s = 0
    s += 1; M["vs_small0"] = s
    for i in range(NT):
        s += 1; M[f"vs_rv{i}"] = s
        if i >= 1:
            s += 1; M[f"vs_clip{i-1}"] = s
            s += 1; M[f"vs_xnb{i-1}"] = s
    s += 1; M[f"vs_clip{NT-1}"] = s
    s += 1; M[f"vs_xnb{NT-1}"] = s
    s += 1; M["vs_bffn"] = s
    s += 1; M["vs_lam"] = s
    s += 1; M["vs_k1"] = s
    s += 1; M["vs_red1"] = s
    s += 1; M["vs_k2"] = s
    s += 1; M["vs_red2"] = s
    s += 1; M["vs_decay"] = s
    s += 1; M["vs_pow"] = s
    s += 1; M["vs_powb"] = s
    s += 1; M["vs_ceff"] = s
    s += 1; M["vs_waRe"] = s
    s += 1; M["vs_waIm"] = s
    s += 1; M["vs_kred"] = s
    s += 1; M["vs_tkb"] = s
    s += 1; M["vs_rtneg"] = s
    for k in range(NB):
        s += 1; M[f"vs_uev{k}"] = s
    s += 1; M["vs_scan"] = s
    s += 1; M["vs_scastb"] = s
    for k in range(NB):
        s += 1; M[f"vs_yt{k}"] = s
    M["vs_final"] = s

    s = 0
    s += 1; M["ss_gen0"] = s
    for i in range(NT):
        s += 1; M[f"ss_aff{i}"] = s
        if i >= 1:
            s += 1; M[f"ss_xq{i-1}"] = s
    s += 1; M[f"ss_xq{NT-1}"] = s
    s += 1; M["ss_int1"] = s
    s += 1; M["ss_sin"] = s
    s += 1; M["ss_int2"] = s
    s += 1; M["ss_cos"] = s
    s += 1; M["ss_exp"] = s
    for i in range(NT):
        s += 1; M[f"ss_fev{i}"] = s
    for k in range(NB):
        s += 1; M[f"ss_gel{k}"] = s
    for i in range(NT):
        s += 1; M[f"ss_sev{i}"] = s
    M["ss_final"] = s

    s = 0
    for i in range(NT):
        s += 1; M[f"ts_ffn{i}"] = s
    for k in range(NB):
        s += 1; M[f"ts_2a{k}"] = s
    for k in range(NB):
        s += 1; M[f"ts_2cm{k}"] = s
        s += 1; M[f"ts_2ct{k}"] = s
    for i in range(NT):
        s += 1; M[f"ts_p3_{i}"] = s
    M["ts_final"] = s

    with nc.Block() as block:

        @block.sync
        def _(sync):
            sync.dma_start(out=c_iota[:, :], in_=iotaE[:, :]).then_inc(Q1, 16)
            sync.dma_start(out=c_j[:, :], in_=jmat[:, :]).then_inc(Q1, 16)
            sync.dma_start(out=c_ones1[:, :], in_=ones1[:, :]).then_inc(Q1, 16)
            sync.dma_start(out=p_ldt[:, :], in_=ldt[:, :]).then_inc(Q1, 16)
            sync.dma_start(out=p_lar[:, :], in_=lar[:, :]).then_inc(Q1, 16)
            sync.dma_start(out=p_aim[:, :], in_=aim[:, :]).then_inc(Q1, 16)
            sync.dma_start(out=p_cre[:, :], in_=cre[:, :]).then_inc(Q1, 16)
            sync.dma_start(out=p_cim[:, :], in_=cim[:, :]).then_inc(Q1, 16)
            sync.dma_start(out=p_isc[:, :], in_=isc[:, :]).then_inc(Q1, 16)
            sync.dma_start(out=p_wsc[:, :], in_=wsc[:, :]).then_inc(Q1, 16)

            for i in range(NT):
                if i >= 3:
                    sync.wait_ge(SS, M[f"ss_aff{i-3}"])
                sync.dma_start(out=x_tl[i % 3],
                               in_=xs[i * 128:(i + 1) * 128, :]).then_inc(XT[i % 3], 16)
                if i >= 1:
                    j = i - 1
                    sync.wait_ge(VS, M[f"vs_xnb{j}"])
                    dst = bass.AP(a1_in, j * 128 * DL,
                                  [[DL, 128], [TL * DL, NCORE], [1, DL]])
                    sync.dma_start(out=dst, in_=xnb_tl[j % 2][:, :]).then_inc(XW[j % 2], 16)
            j = NT - 1
            sync.wait_ge(VS, M[f"vs_xnb{j}"])
            dst = bass.AP(a1_in, j * 128 * DL, [[DL, 128], [TL * DL, NCORE], [1, DL]])
            sync.dma_start(out=dst, in_=xnb_tl[j % 2][:, :]).then_inc(XW[j % 2], 16)

            sync.wait_ge(VS, M["vs_small0"])
            zsrc = bass.AP(z128, 0, [[128, 128], [0, 64], [1, 128]])
            zdst = bass.AP(s_scr, 0, [[64 * U, 128], [128, 64], [1, 128]])
            sync.dma_start(out=zdst, in_=zsrc).then_inc(Q1, 16)
            sync.wait_ge(Q1, M["q1_params"])
            sync.wait_ge(SDX[0], M[f"sdx_{NT-2}"])
            sync.wait_ge(SDX[1], M[f"sdx_{NT-1}"])
            for k in range(8):
                sync.dma_start(out=xdqT[:, k * TL:(k + 1) * TL],
                               in_=xq_scr[:, k * 128:(k + 1) * 128],
                               transpose=True).then_inc(Q1, 16)

            sync.wait_ge(Q1, M["q1_tr15"])
            sync.wait_ge(GPC, M["gpc_cc1"])
            for j in range(16):
                src = bass.AP(a1_out, j * 8 * 128 * DL,
                              [[DL, 128], [128 * DL, 8], [1, DL]])
                sync.dma_start(out=x_all[:, j * 1024:(j + 1) * 1024],
                               in_=src).then_inc(Q1, 16)
            sync.wait_ge(GS, M["gs_waTim"])
            for g in range(NGRP):
                if g >= 2:
                    sync.wait_ge(TS, M[f"ts_2a{4*(g-1)-1}"])
                src = bass.AP(waT, g * GC * 64, [[DL * 64, T], [1, GC * 64]])
                sync.dma_start(out=wa_b[g % 2][:, :], in_=src).then_inc(QW[g % 2], 16)

            sync.wait_ge(GS, M["gs_rT"])
            sync.dma_start(out=m_are[:, :], in_=rTre[:, :]).then_inc(Q1, 16)
            sync.dma_start(out=m_aim[:, :], in_=rTim[:, :]).then_inc(Q1, 16)
            sync.wait_ge(Q1, M["q1_mult"])
            sync.wait_ge(SU[0], M[f"su_{NGRP-2}"])
            sync.wait_ge(SU[1], M[f"su_{NGRP-1}"])
            sync.wait_ge(VS, M["vs_rtneg"])
            for srcT, dstT in ((uReD, gA), (uImD, gB)):
                srcp = bass.AP(srcT, 0, [[DL * CH, 128], [1, DL * CH]])
                dstp = bass.AP(dstT, 0, [[GW, 128], [1, DL * CH]])
                sync.dma_start(out=dstp, in_=srcp).then_inc(Q1, 16)

            sync.wait_ge(VS, M["vs_scastb"])
            for comp, srcoff in ((0, 0), (32, GEN)):
                for dh in range(2):
                    srcp = bass.AP(powb, srcoff + dh * 64 * 32,
                                   [[2 * GEN, 128], [32, 64], [1, 31]])
                    dstp = bass.AP(s_scr, comp * U + 1 + dh * 64 * 64 * U,
                                   [[32, 128], [64 * U, 64], [1, 31]])
                    sync.dma_start(out=dstp, in_=srcp).then_inc(Q1, 16)
            sync.wait_ge(Q1, M["q1_sscrw"])
            sync.wait_ge(TS, M[f"ts_2a{NB-1}"])
            sync.wait_ge(GS, M["gs_kpad"])
            for g in range(NGRP):
                if g >= 1:
                    sync.wait_ge(TS, M[f"ts_2cm{4*g-1}"])
                src = bass.AP(kpadR, g * GC * 256, [[1, 128], [256, GC], [1, 128]])
                sync.dma_start(out=hk_b[g % 2][:, :], in_=src).then_inc(Q1, 16)
                for kk in range(2):
                    src = bass.AP(eT1, (g * 2 + kk) * 8 * T * 128,
                                  [[128, 8 * T], [1, 128]])
                    sync.dma_start(out=e_b[g % 2][:, kk * 1024:(kk + 1) * 1024],
                                   in_=src, transpose=True).then_inc(Q1, 16)
                src = bass.AP(s_scr, g * GC * 64 * U, [[U, 64], [64 * U, GC], [1, U]])
                sync.dma_start(out=sm_b[g % 2][:, :], in_=src).then_inc(Q1, 16)

            sync.wait_ge(GPC, M["gpc_cc2"])
            for i in range(NT):
                if i >= 4:
                    sync.wait_ge(TS, M[f"ts_p3_{i-4}"])
                    sync.wait_ge(GPC, M[f"gpc_comb{i-4}"])
                elif i >= 2:
                    pass
                src = bass.AP(a2_out, i * 128, [[TL, 128], [128 * TL, 8], [1, 128]])
                sync.dma_start(out=sst_tl[i % 4][:, :], in_=src).then_inc(QP[i % 4], 16)
                if i >= 2:
                    sync.wait_ge(GPC, M[f"gpc_comb{i-2}"])
                sync.dma_start(out=xr_tl[i % 2],
                               in_=xs[i * 128:(i + 1) * 128, :]).then_inc(QP[i % 4], 16)
                sync.wait_ge(GWS[i % 2], M[f"gw_{i}"])
                sync.dma_start(out=gg_sb[i % 4],
                               in_=g_scr[i * 128:(i + 1) * 128, :]).then_inc(QP[i % 4], 16)

        @block.gpsimd
        def _(gpsimd):
            for k in range(8):
                gpsimd.dma_start(out=wft_b[:, k * D:(k + 1) * D],
                                 in_=WFTd[k * 128:(k + 1) * 128, :]).then_inc(GPD, 16)
            gpsimd.dma_start(out=bffn_row[:, :], in_=bffn_r[:, :]).then_inc(GPD, 16)
            gpsimd.wait_ge(GPD, M["gpd_wft"])
            gpsimd.wait_ge(XW[0], M[f"xw_{NT-2}"])
            gpsimd.wait_ge(XW[1], M[f"xw_{NT-1}"])
            with allow_ici_all_to_all():
                gpsimd.collective_compute(
                    "AllToAll", ALU.bypass,
                    replica_groups=[list(range(NCORE))],
                    ins=[a1_in.ap().opt()],
                    outs=[a1_out.ap().opt()],
                ).then_inc(GPC, 1)
            gpsimd.wait_ge(SYP[0], M[f"sy_{NB-2}"])
            gpsimd.wait_ge(SYP[1], M[f"sy_{NB-1}"])
            src = bass.AP(y_scr, 0, [[TL, NCORE], [TOKS, DL], [1, TL]])
            dst = bass.AP(a2_in, 0, [[DL * TL, NCORE], [TL, DL], [1, TL]])
            gpsimd.dma_start(out=dst, in_=src).then_inc(GPD, 16)
            gpsimd.wait_ge(GPD, M["gpd_yshuf"])
            gpsimd.wait_ge(SS, M[f"ss_fev{NT-1}"])
            for k in range(8):
                gpsimd.dma_start(out=wt_b[:, k * D:(k + 1) * D],
                                 in_=WTd[k * 128:(k + 1) * 128, :]).then_inc(GPD, 16)
            gpsimd.dma_start(out=battn_bb[:, :], in_=battn_r[:, :]).then_inc(GPD, 16)
            with allow_ici_all_to_all():
                gpsimd.collective_compute(
                    "AllToAll", ALU.bypass,
                    replica_groups=[list(range(NCORE))],
                    ins=[a2_in.ap().opt()],
                    outs=[a2_out.ap().opt()],
                ).then_inc(GPC, 1)
            for i in range(NT):
                gpsimd.wait_ge(SS, M[f"ss_sev{i}"])
                gpsimd.wait_ge(QP[i % 4], M[f"qp_{i}"])
                if i >= 2:
                    gpsimd.wait_ge(GPD, M[f"gpd_out{i-2}"])
                gpsimd.tensor_tensor(out=o_sb[i % 2], in0=s_sb[i % 4],
                                     in1=gg_sb[i % 4], op=ALU.mult).then_inc(GPC, 1)
                gpsimd.tensor_tensor(out=o_sb[i % 2], in0=o_sb[i % 2],
                                     in1=xr_tl[i % 2], op=ALU.add).then_inc(GPC, 1)
                gpsimd.dma_start(out=out_t[i * 128:(i + 1) * 128, :],
                                 in_=o_sb[i % 2]).then_inc(GPD, 16)
            gpsimd.wait_ge(GPD, M["gpd_final"])

        @block.vector
        def _(vector):
            vxc = [0]

            def vg(instr):
                vxc[0] += 1
                instr.then_inc(VX, 1)
                vector.wait_ge(VX, vxc[0])
                return instr

            vector.wait_ge(Q1, M["q1_params"])
            vector.reciprocal(out=v_invs[:, :], in_=p_isc[:, :])
            vg(vector.tensor_mul(v_s2[:, :], p_isc[:, :], p_wsc[:, :]))
            vg(vector.reciprocal(out=v_rs2[:, :], in_=v_s2[:, :]))
            vector.memset(z128[:, :], 0.0).then_inc(VS, 1)

            def emit_clip(j):
                vector.wait_ge(SS, M[f"ss_aff{j}"])
                vg(vector.tensor_scalar(out=xq1_tl[j % 2], in0=xn_tl[j % 2],
                                        scalar1=v_invs[:, 0:1], scalar2=448.0,
                                        op0=ALU.mult, op1=ALU.min))
                vector.tensor_scalar(out=xq1_tl[j % 2], in0=xq1_tl[j % 2],
                                     scalar1=-448.0, scalar2=0.5,
                                     op0=ALU.max, op1=ALU.mult).then_inc(VS, 1)
                if j >= 2:
                    vector.wait_ge(XW[j % 2], M[f"xw_{j-2}"])
                vector.tensor_copy(out=xnb_tl[j % 2][:, :],
                                   in_=xn_tl[j % 2]).then_inc(VS, 1)

            for i in range(NT):
                vector.wait_ge(XT[i % 3], M[f"xt_{i}"])
                xt = x_tl[i % 3]
                vector.bn_stats(out=s_bn[:, 0:6], in_=xt[:, 0:512])
                vg(vector.bn_stats(out=s_bn[:, 6:12], in_=xt[:, 512:1024]))
                vector.bn_aggr(out=s_mv1[:, :], in_=s_bn[:, 0:6])
                vg(vector.bn_aggr(out=s_mv2[:, :], in_=s_bn[:, 6:12]))
                vector.tensor_sub(s_tmp[:, :], s_mv1[:, 0:1], s_mv2[:, 0:1])
                vg(vector.tensor_add(s_mv1[:, :], s_mv1[:, :], s_mv2[:, :]))
                vector.tensor_mul(s_tmp[:, :], s_tmp[:, :], s_tmp[:, :])
                vg(vector.tensor_scalar_mul(out=s_mv1[:, :], in0=s_mv1[:, :],
                                            scalar1=0.5))
                vg(vector.tensor_scalar_mul(out=s_nmr[i % 2][:, :],
                                            in0=s_mv1[:, 0:1], scalar1=-1.0))
                vg(vector.scalar_tensor_tensor(out=s_var[:, :], in0=s_tmp[:, :],
                                               scalar=0.25, in1=s_mv1[:, 1:2],
                                               op0=ALU.mult, op1=ALU.add))
                vg(vector.tensor_scalar_add(out=s_var[:, :], in0=s_var[:, :],
                                            scalar1=1e-5))
                vector.reciprocal(out=s_rstd[i % 2][:, :],
                                  in_=s_var[:, :]).then_inc(VS, 1)
                if i >= 1:
                    emit_clip(i - 1)
            emit_clip(NT - 1)

            vector.wait_ge(GPD, M["gpd_wft"])
            vector.tensor_scalar_mul(out=bffn_sb[:, :], in0=bffn_row[:, :],
                                     scalar1=v_rs2[0:1, 0:1]).then_inc(VS, 1)
            vector.wait_ge(SS, M["ss_gen0"])
            vector.tensor_scalar(out=d_lre[:, :], in0=d_elar[:, :],
                                 scalar1=d_dt[:, 0:1], scalar2=-1.0,
                                 op0=ALU.mult, op1=ALU.mult)
            vector.tensor_scalar(out=d_lim[:, :], in0=p_aim[:, :],
                                 scalar1=d_dt[:, 0:1], scalar2=None, op0=ALU.mult)
            vector.tensor_mul(d_t1[:, :], d_elar[:, :], d_elar[:, :])
            vg(vector.tensor_mul(d_t2[:, :], p_aim[:, :], p_aim[:, :]))
            vg(vector.tensor_add(d_t1[:, :], d_t1[:, :], d_t2[:, :]))
            vector.reciprocal(out=d_ia2[:, :], in_=d_t1[:, :]).then_inc(VS, 1)
            gAv = gA[:, 0:GEN]
            gBv = gB[:, 0:GEN]
            gCv = gC[:, 0:GEN]
            lim_bc = bass.AP(d_lim, 0, [[NM, 128], [0, NE], [1, NM]])
            lre_bc = bass.AP(d_lre, 0, [[NM, 128], [0, NE], [1, NM]])
            iot_bc = bass.AP(c_iota, 0, [[NE, 128], [1, NE], [0, NM]])
            vector.tensor_tensor(out=gAv, in0=iot_bc, in1=lim_bc, op=ALU.mult)
            vector.wait_ge(SS, M[f"ss_xq{NT-1}"])
            vector.tensor_scalar(out=gBv, in0=gAv, scalar1=INV2PI, scalar2=None,
                                 op0=ALU.mult).then_inc(VS, 1)
            vector.wait_ge(SS, M["ss_int1"])
            vector.scalar_tensor_tensor(out=gBv, in0=gBv, scalar=-TWO_PI,
                                        in1=gAv, op0=ALU.mult,
                                        op1=ALU.add).then_inc(VS, 1)
            vector.wait_ge(SDX[0], M[f"sdx_{NT-2}"])
            vector.wait_ge(SDX[1], M[f"sdx_{NT-1}"])
            vector.tensor_scalar_add(out=gCv, in0=gAv, scalar1=PI / 2.0)
            vector.tensor_scalar(out=gAv, in0=gCv, scalar1=INV2PI, scalar2=None,
                                 op0=ALU.mult).then_inc(VS, 1)
            vector.wait_ge(SS, M["ss_int2"])
            vector.scalar_tensor_tensor(out=gCv, in0=gAv, scalar=-TWO_PI,
                                        in1=gCv, op0=ALU.mult,
                                        op1=ALU.add).then_inc(VS, 1)
            vector.wait_ge(SS, M["ss_cos"])
            vector.tensor_tensor(out=gAv, in0=iot_bc, in1=lre_bc,
                                 op=ALU.mult).then_inc(VS, 1)
            vector.wait_ge(SS, M["ss_exp"])
            vector.tensor_mul(gCv, gAv, gCv)
            vector.tensor_mul(gBv, gAv, gBv).then_inc(VS, 1)
            vector.tensor_copy(out=powCb, in_=gCv)
            vector.tensor_copy(out=powSb, in_=gBv).then_inc(VS, 1)
            sl = slice(127 * NM, 128 * NM)
            vector.tensor_scalar_add(out=d_t1[:, :], in0=gC[:, sl], scalar1=-1.0)
            vg(vector.tensor_copy(out=d_t2[:, :], in_=gB[:, sl]))
            vector.tensor_mul(d_wre[:, :], d_t1[:, :], d_elar[:, :])
            vg(vector.tensor_mul(d_t3[:, :], d_t2[:, :], p_aim[:, :]))
            vg(vector.tensor_sub(d_wre[:, :], d_t3[:, :], d_wre[:, :]))
            vg(vector.tensor_mul(d_wre[:, :], d_wre[:, :], d_ia2[:, :]))
            vector.tensor_mul(d_wim[:, :], d_t2[:, :], d_elar[:, :])
            vg(vector.tensor_mul(d_t3[:, :], d_t1[:, :], p_aim[:, :]))
            vg(vector.tensor_add(d_wim[:, :], d_wim[:, :], d_t3[:, :]))
            vg(vector.tensor_scalar_mul(out=d_wim[:, :], in0=d_wim[:, :],
                                        scalar1=-1.0))
            vg(vector.tensor_mul(d_wim[:, :], d_wim[:, :], d_ia2[:, :]))
            vector.tensor_mul(d_cfr[:, :], p_cre[:, :], d_wre[:, :])
            vg(vector.tensor_mul(d_t3[:, :], p_cim[:, :], d_wim[:, :]))
            vg(vector.tensor_sub(d_cfr[:, :], d_cfr[:, :], d_t3[:, :]))
            vector.tensor_mul(d_cfi[:, :], p_cre[:, :], d_wim[:, :])
            vg(vector.tensor_mul(d_t3[:, :], p_cim[:, :], d_wre[:, :]))
            vg(vector.tensor_add(d_cfi[:, :], d_cfi[:, :], d_t3[:, :]))
            vector.engine_nop().then_inc(VS, 1)
            cfr_bc = bass.AP(d_cfr, 0, [[NM, 128], [0, 128], [1, NM]])
            cfi_bc = bass.AP(d_cfi, 0, [[NM, 128], [0, 128], [1, NM]])
            A32 = bass.AP(gA, NM, [[GW, 128], [1, 128 * NM]])
            B32 = bass.AP(gB, NM, [[GW, 128], [1, 128 * NM]])
            C32 = bass.AP(gC, NM, [[GW, 128], [1, 128 * NM]])
            I32 = bass.AP(gI, 0, [[GW, 128], [1, 128 * NM]]).bitcast(F32)
            waRe_o = bass.AP(powb, 0, [[2 * GEN, 128], [64, 128], [1, NM]])
            waIm_o = bass.AP(powb, NM, [[2 * GEN, 128], [64, 128], [1, NM]])
            vector.wait_ge(GS, M["gs_eT"])
            vector.scalar_tensor_tensor(out=A32, in0=C32, scalar=2.0,
                                        in1=cfr_bc, op0=ALU.mult, op1=ALU.mult)
            vector.scalar_tensor_tensor(out=C32, in0=C32, scalar=-2.0,
                                        in1=cfi_bc, op0=ALU.mult, op1=ALU.mult)
            vg(vector.scalar_tensor_tensor(out=I32, in0=B32, scalar=2.0,
                                           in1=cfi_bc, op0=ALU.mult, op1=ALU.mult))
            vector.tensor_tensor(out=waRe_o, in0=A32, in1=I32,
                                 op=ALU.subtract).then_inc(VS, 1)
            vg(vector.scalar_tensor_tensor(out=I32, in0=B32, scalar=-2.0,
                                           in1=cfr_bc, op0=ALU.mult, op1=ALU.mult))
            vector.tensor_tensor(out=waIm_o, in0=I32, in1=C32,
                                 op=ALU.add).then_inc(VS, 1)
            vector.tensor_reduce(out=t_K[:, 1:NE],
                                 in_=bass.AP(powb, 0, [[2 * GEN, 128], [64, 128], [1, 32]]),
                                 axis=mybir.AxisListType.X,
                                 op=ALU.add).then_inc(VS, 1)
            vector.tensor_copy(out=t_KB[:, :], in_=t_K[:, :]).then_inc(VS, 1)
            vector.tensor_scalar_mul(out=d_aimn[:, :], in0=gB[:, 0:NM],
                                     scalar1=-1.0).then_inc(VS, 1)

            for k in range(NB):
                g = k // 4
                vector.wait_ge(TS, M[f"ts_2a{k}"])
                chb = (k % 4) * 4
                src = bass.AP(psUB[k % 4], 0, [[512, 64], [128, 4], [32, 4], [1, 32]])
                dst = u_sb_ap(g, chb * CH,
                              [[GW, 64], [CH, 4], [GC * CH, 4], [1, CH]])
                vector.tensor_copy(out=dst, in_=src).then_inc(VS, 1)
                if k % 4 == 3 and g >= 1 and g < NGRP - 1:
                    vector.wait_ge(SU[g % 2], M[f"su_{g-1}"])

            vector.wait_ge(Q1, M["q1_scanrd"])
            last = None
            for c in range(1, CH):
                sre_p = bass.AP(gA, c - 1, [[GW, 128], [32, DL]])
                sim_p = bass.AP(gB, c - 1, [[GW, 128], [32, DL]])
                sre_n = bass.AP(gA, c, [[GW, 128], [32, DL]])
                sim_n = bass.AP(gB, c, [[GW, 128], [32, DL]])
                vector.tensor_mul(w_t1[:, :], m_are[:, :], sre_p)
                vector.tensor_mul(w_t2[:, :], m_aim[:, :], sim_p)
                vector.tensor_mul(w_t3[:, :], m_aim[:, :], sre_p)
                vg(vector.tensor_mul(w_t4[:, :], m_are[:, :], sim_p))
                vector.tensor_sub(w_t1[:, :], w_t1[:, :], w_t2[:, :])
                vg(vector.tensor_add(w_t3[:, :], w_t3[:, :], w_t4[:, :]))
                vector.tensor_add(sre_n, sre_n, w_t1[:, :])
                last = vector.tensor_add(sim_n, sim_n, w_t3[:, :])
                vg(last)
            vector.engine_nop().then_inc(VS, 1)
            vector.wait_ge(GS, M["gs_waTim"])
            vector.tensor_copy(out=scbRe, in_=gA[:, 0:DL * CH])
            vector.tensor_copy(out=scbIm, in_=gB[:, 0:DL * CH]).then_inc(VS, 1)

            for k in range(NB):
                vector.wait_ge(TS, M[f"ts_2ct{k}"])
                if k >= 2:
                    vector.wait_ge(SYP[k % 2], M[f"sy_{k-2}"])
                vector.tensor_copy(out=yt_sb[k % 2][:, :],
                                   in_=psT[k % 2][:, :]).then_inc(VS, 1)

        @block.scalar
        def _(scalar):
            sxc = [0]

            def sg(instr):
                sxc[0] += 1
                instr.then_inc(SX, 1)
                scalar.wait_ge(SX, sxc[0])
                return instr

            scalar.wait_ge(Q1, M["q1_params"])
            scalar.activation(out=d_dt[:, :], in_=p_ldt[:, :], func=AF.Exp)
            scalar.activation(out=d_elar[:, :], in_=p_lar[:, :],
                              func=AF.Exp).then_inc(SS, 1)
            for i in range(NT):
                scalar.wait_ge(VS, M[f"vs_rv{i}"])
                if i >= 2:
                    scalar.wait_ge(XW[i % 2], M[f"xw_{i-2}"])
                sg(scalar.activation(out=s_rstd[i % 2][:, :],
                                     in_=s_rstd[i % 2][:, :], func=AF.Sqrt))
                sg(scalar.activation(out=s_nmr[i % 2][:, :],
                                     in_=s_nmr[i % 2][:, :],
                                     func=AF.Identity,
                                     scale=s_rstd[i % 2][:, 0:1]))
                scalar.activation(out=xn_tl[i % 2], in_=x_tl[i % 3],
                                  func=AF.Identity, scale=s_rstd[i % 2][:, 0:1],
                                  bias=s_nmr[i % 2][:, 0:1]).then_inc(SS, 1)
                if i >= 1:
                    j = i - 1
                    scalar.wait_ge(VS, M[f"vs_clip{j}"])
                    if j >= 2:
                        scalar.wait_ge(SDX[j % 2], M[f"sdx_{j-2}"])
                    sg(scalar.activation(out=f8t[j % 2][:, :], in_=xq1_tl[j % 2],
                                         func=AF.Copy))
                    scalar.activation(out=xqbt[j % 2], in_=f8t[j % 2][:, :],
                                      func=AF.Copy, scale=2.0).then_inc(SS, 1)
                    sg(scalar.activation(out=s_dmy[:, :], in_=s_dmy[:, :],
                                         func=AF.Copy))
                    scalar.dma_start(out=xq_scr[j * 128:(j + 1) * 128, :],
                                     in_=xqbt[j % 2]).then_inc(SDX[j % 2], 16)
            j = NT - 1
            scalar.wait_ge(VS, M[f"vs_clip{j}"])
            sg(scalar.activation(out=f8t[j % 2][:, :], in_=xq1_tl[j % 2],
                                 func=AF.Copy))
            scalar.activation(out=xqbt[j % 2], in_=f8t[j % 2][:, :],
                              func=AF.Copy, scale=2.0).then_inc(SS, 1)
            sg(scalar.activation(out=s_dmy[:, :], in_=s_dmy[:, :], func=AF.Copy))
            scalar.dma_start(out=xq_scr[j * 128:(j + 1) * 128, :],
                             in_=xqbt[j % 2]).then_inc(SDX[j % 2], 16)
            gAv = gA[:, 0:GEN]
            gBv = gB[:, 0:GEN]
            gCv = gC[:, 0:GEN]
            gIi = gI[:, 0:GEN]
            scalar.wait_ge(VS, M["vs_k1"])
            scalar.activation(out=gIi, in_=gBv, func=AF.Copy)
            scalar.activation(out=gBv, in_=gIi, func=AF.Copy).then_inc(SS, 1)
            scalar.wait_ge(VS, M["vs_red1"])
            scalar.activation(out=gBv, in_=gBv, func=AF.Sin).then_inc(SS, 1)
            scalar.wait_ge(VS, M["vs_k2"])
            scalar.activation(out=gIi, in_=gAv, func=AF.Copy)
            scalar.activation(out=gAv, in_=gIi, func=AF.Copy).then_inc(SS, 1)
            scalar.wait_ge(VS, M["vs_red2"])
            scalar.activation(out=gCv, in_=gCv, func=AF.Sin).then_inc(SS, 1)
            scalar.wait_ge(VS, M["vs_decay"])
            scalar.activation(out=gAv, in_=gAv, func=AF.Exp).then_inc(SS, 1)
            scalar.wait_ge(VS, M["vs_powb"])
            for comp, srcoff in ((0, 0), (32, GEN)):
                for jh in range(2):
                    srcp = bass.AP(powb, srcoff + jh * 64 * 32,
                                   [[2 * GEN, 128], [32, 64], [1, 32]])
                    dstp = bass.AP(eT1, comp + jh * 64 * 128,
                                   [[T * 128, 128], [128, 64], [1, 32]])
                    scalar.dma_start(out=dstp, in_=srcp).then_inc(GS, 16)
            scalar.wait_ge(GS, M["gs_eT"])
            scalar.wait_ge(VS, M["vs_waIm"])
            for th in range(2):
                srcp = bass.AP(powb, th * 64 * 64, [[2 * GEN, 128], [1, 64 * 64]])
                dstp = bass.AP(waT, th * 64 * DL * 64,
                               [[64, 128], [DL * 64, 64], [1, 64]])
                scalar.dma_start(out=dstp, in_=srcp).then_inc(GS, 16)
            scalar.wait_ge(GS, M["gs_waTim"])
            scalar.wait_ge(VS, M["vs_tkb"])
            srcp = bass.AP(t_KB, 1, [[NE, 128], [1, 128]])
            scalar.dma_start(out=kpadR[:, 0:128], in_=srcp).then_inc(GS, 16)
            scalar.dma_start(out=kpadR[:, 128:256], in_=z128[:, :]).then_inc(GS, 16)
            scalar.wait_ge(GS, M["gs_kpad"])
            scalar.wait_ge(VS, M["vs_rtneg"])
            for bb4 in range(4):
                dstp = bass.AP(rTre, bb4 * DL, [[1, 128], [4 * DL, 32]])
                scalar.dma_start(out=dstp, in_=gC[:, 0:32],
                                 allow_slow_non_contiguous=True).then_inc(GS, 16)
            for bb4 in range(4):
                dstp = bass.AP(rTim, bb4 * DL, [[1, 128], [4 * DL, 32]])
                scalar.dma_start(out=dstp, in_=d_aimn[:, :],
                                 allow_slow_non_contiguous=True).then_inc(GS, 16)
            scalar.wait_ge(GS, M["gs_rT"])
            for i in range(NT):
                scalar.wait_ge(TS, M[f"ts_ffn{i}"])
                if i >= 2:
                    scalar.wait_ge(GWS[i % 2], M[f"gw_{i-2}"])
                pF = psF if i % 2 == 0 else psY
                scalar.activation(out=gtmp[i % 2][:, 0:512],
                                  in_=pF[0][:, :], func=AF.Silu,
                                  scale=v_s2[:, 0:1])
                scalar.activation(out=gtmp[i % 2][:, 512:1024],
                                  in_=pF[1][:, :], func=AF.Silu,
                                  scale=v_s2[:, 0:1]).then_inc(SS, 1)
                sg(scalar.activation(out=s_dmy[:, :], in_=s_dmy[:, :],
                                     func=AF.Copy))
                scalar.dma_start(out=g_scr[i * 128:(i + 1) * 128, :],
                                 in_=gtmp[i % 2]).then_inc(GWS[i % 2], 16)
            for g in range(NGRP):
                scalar.wait_ge(VS, M[f"vs_uev{4*g+3}"])
                if g >= 2:
                    scalar.wait_ge(SU[g % 2], M[f"su_{g-2}"])
                for comp, dstT in ((0, uReD), (32, uImD)):
                    srcp = u_sb_ap(g, comp * GW,
                                   [[GW, 32], [GC * CH, 4], [1, GC * CH]])
                    dstp = bass.AP(dstT, g * GC * CH,
                                   [[4 * DL * CH, 32], [DL * CH, 4],
                                    [1, GC * CH]])
                    scalar.dma_start(out=dstp, in_=srcp).then_inc(SU[g % 2], 16)
            for k in range(NB):
                scalar.wait_ge(TS, M[f"ts_2cm{k}"])
                scalar.activation(out=yg_sb[k % 2][:, :], in_=psY[k % 2][:, :],
                                  func=AF.Gelu).then_inc(SS, 1)
                if k >= 1:
                    j = k - 1
                    scalar.wait_ge(VS, M[f"vs_yt{j}"])
                    if j >= 2:
                        scalar.wait_ge(SYP[j % 2], M[f"sy_{j-2}"])
                    dst = bass.AP(y_scr, 4 * j * TOKS,
                                  [[128, U], [TOKS, 4], [1, 128]])
                    scalar.dma_start(out=dst,
                                     in_=yt_sb[j % 2][:, :]).then_inc(SYP[j % 2], 16)
            j = NB - 1
            scalar.wait_ge(VS, M[f"vs_yt{j}"])
            scalar.wait_ge(SYP[j % 2], M[f"sy_{j-2}"])
            dst = bass.AP(y_scr, 4 * j * TOKS, [[128, U], [TOKS, 4], [1, 128]])
            scalar.dma_start(out=dst, in_=yt_sb[j % 2][:, :]).then_inc(SYP[j % 2], 16)
            scalar.wait_ge(Q1, M["q1_sscrw"])
            for i in range(NT):
                scalar.wait_ge(TS, M[f"ts_p3_{i}"])
                if i >= 4:
                    scalar.wait_ge(GPC, M[f"gpc_comb{i-4}"])
                pF = psF if i % 2 == 0 else psY
                scalar.activation(out=s_sb[i % 4][:, 0:512], in_=pF[0][:, :],
                                  func=AF.Copy)
                scalar.activation(out=s_sb[i % 4][:, 512:1024], in_=pF[1][:, :],
                                  func=AF.Copy).then_inc(SS, 1)

        @block.tensor
        def _(tensor):
            tensor.wait_ge(Q1, M["q1_tr15"])
            tensor.wait_ge(GPD, M["gpd_wft"])
            tensor.wait_ge(VS, M["vs_bffn"])
            for i in range(NT):
                if i >= 2:
                    tensor.wait_ge(SS, M[f"ss_fev{i-2}"])
                pF = psF if i % 2 == 0 else psY
                for kk in range(8):
                    lhsT = xdqT[:, kk * TL + i * 128: kk * TL + (i + 1) * 128]
                    for h in range(2):
                        tensor.matmul(pF[h][:, :], lhsT,
                                      wft_b[:, kk * D + h * 512: kk * D + h * 512 + 512],
                                      start=(kk == 0), stop=False)
                last = None
                for h in range(2):
                    last = tensor.matmul(pF[h][:, :], c_ones1[:, :],
                                         bffn_sb[0:1, h * 512:h * 512 + 512],
                                         start=False, stop=True)
                last.then_inc(TS, 1)

            tensor.wait_ge(Q1, M["q1_xall"])
            tensor.wait_ge(SS, M[f"ss_fev{NT-1}"])
            for k in range(NB):
                g = k // 4
                tensor.wait_ge(QW[g % 2], M[f"qw_{g}"])
                if k >= 4:
                    tensor.wait_ge(VS, M[f"vs_uev{k-4}"])
                last = None
                for ci in range(4):
                    ch = (k % 4) * 4 + ci
                    lhsT = wa_b[g % 2][:, ch * 64:(ch + 1) * 64]
                    rhs = bass.AP(x_all, g * GC + ch, [[U * 128, 128], [128, U]])
                    last = tensor.matmul(psUB[k % 4][0:64, ci * 128:(ci + 1) * 128],
                                         lhsT, rhs, start=True, stop=True)
                last.then_inc(TS, 1)

            tensor.wait_ge(VS, M["vs_scan"])
            for k in range(NB):
                g = k // 4
                tensor.wait_ge(Q1, M[f"q1_hk{g}"])
                tensor.wait_ge(Q1, M[f"q1_e{g}"])
                tensor.wait_ge(Q1, M[f"q1_sm{g}"])
                if k >= 2:
                    tensor.wait_ge(SS, M[f"ss_gel{k-2}"])
                last = None
                for ci in range(4):
                    ch = (k % 4) * 4 + ci
                    lhsT = hk_b[g % 2][:, ch * 128:(ch + 1) * 128]
                    rhs = bass.AP(x_all, g * GC + ch, [[U * 128, 128], [128, U]])
                    tensor.matmul(psY[k % 2][:, ci * 128:(ci + 1) * 128], lhsT, rhs,
                                  start=True, stop=False)
                    lhsT = e_b[g % 2][0:64, ch * 128:(ch + 1) * 128]
                    rhs = sm_b[g % 2][0:64, ch * U:(ch + 1) * U]
                    last = tensor.matmul(psY[k % 2][:, ci * 128:(ci + 1) * 128],
                                         lhsT, rhs, start=False, stop=True)
                last.then_inc(TS, 1)
                tensor.wait_ge(SS, M[f"ss_gel{k}"])
                if k >= 2:
                    tensor.wait_ge(VS, M[f"vs_yt{k-2}"])
                last = None
                for ci in range(4):
                    last = tensor.transpose(psT[k % 2][:, ci * 128:(ci + 1) * 128],
                                            yg_sb[k % 2][:, ci * 128:(ci + 1) * 128],
                                            c_j[:, :])
                last.then_inc(TS, 1)

            tensor.wait_ge(GPC, M["gpc_cc2"])
            tensor.wait_ge(GPD, M["gpd_wt"])
            tensor.wait_ge(SS, M[f"ss_fev{NT-1}"])
            tensor.wait_ge(GPD, M["gpd_battn"])
            for i in range(NT):
                tensor.wait_ge(QP[i % 4], M[f"qp_{i}"])
                if i >= 2:
                    tensor.wait_ge(SS, M[f"ss_sev{i-2}"])
                pF = psF if i % 2 == 0 else psY
                for kk in range(8):
                    lhsT = sst_tl[i % 4][:, kk * 128:(kk + 1) * 128]
                    for h in range(2):
                        tensor.matmul(pF[h][:, :], lhsT,
                                      wt_b[:, kk * D + h * 512: kk * D + h * 512 + 512],
                                      start=(kk == 0), stop=False)
                last = None
                for h in range(2):
                    last = tensor.matmul(pF[h][:, :], c_ones1[:, :],
                                         battn_bb[0:1, h * 512:h * 512 + 512],
                                         start=False, stop=True)
                last.then_inc(TS, 1)

    ctx.close()
    return nc


def kernel(L0_attn_x, L0_attn_log_dt, L0_attn_log_A_real, L0_attn_A_imag,
           L0_attn_C_real, L0_attn_C_imag, L0_attn_W, L0_attn_b,
           L0_ffn_weight_fp8, L0_ffn_bias, L0_ffn_input_scale,
           L0_ffn_weight_scale):
    x = np.ascontiguousarray(np.asarray(L0_attn_x, dtype=np.float32))
    WTh = np.ascontiguousarray(np.asarray(L0_attn_W, dtype=np.float32).T)
    WFTh = np.ascontiguousarray(np.asarray(L0_ffn_weight_fp8, dtype=np.float32).T)
    battn = np.ascontiguousarray(np.asarray(L0_attn_b, np.float32).reshape(1, D))
    bffn = np.ascontiguousarray(np.asarray(L0_ffn_bias, np.float32).reshape(1, D))
    iscale = np.ascontiguousarray(np.broadcast_to(
        np.asarray(L0_ffn_input_scale, np.float32).reshape(1, 1), (DL, 1)))
    wscale = np.ascontiguousarray(np.broadcast_to(
        np.asarray(L0_ffn_weight_scale, np.float32).reshape(1, 1), (DL, 1)))
    iota = np.ascontiguousarray(np.broadcast_to(
        np.arange(128, -1, -1, dtype=np.float32), (DL, NE)))
    jm = np.eye(128, dtype=np.float32)[::-1].copy().astype(ml_dtypes.bfloat16)
    o1 = np.ones((1, 128), dtype=np.float32).astype(ml_dtypes.bfloat16)

    ldt_a = np.asarray(L0_attn_log_dt, np.float32)
    lar_a = np.asarray(L0_attn_log_A_real, np.float32)
    aim_a = np.asarray(L0_attn_A_imag, np.float32)
    cre_a = np.asarray(L0_attn_C_real, np.float32)
    cim_a = np.asarray(L0_attn_C_imag, np.float32)

    nc = build_program()
    in_maps = []
    for core in range(NCORE):
        b, h = core // 2, core % 2
        sl = slice(DL * core, DL * (core + 1))
        in_maps.append(dict(
            xs=np.ascontiguousarray(x[b, h * TL:(h + 1) * TL, :]),
            ldt=np.ascontiguousarray(ldt_a[sl].reshape(DL, 1)),
            lar=np.ascontiguousarray(lar_a[sl]),
            aim=np.ascontiguousarray(aim_a[sl]),
            cre=np.ascontiguousarray(cre_a[sl]),
            cim=np.ascontiguousarray(cim_a[sl]),
            WT=WTh, WFT=WFTh, battn_r=battn, bffn_r=bffn,
            isc=iscale, wsc=wscale, iotaE=iota, jmat=jm, ones1=o1,
        ))
    res = run_bass_kernel_spmd(nc, in_maps, core_ids=list(range(NCORE)))
    out = np.empty((BB, L, D), dtype=np.float32)
    for core in range(NCORE):
        b, h = core // 2, core % 2
        out[b, h * TL:(h + 1) * TL, :] = res.results[core]["out"]
    return out


if __name__ == "__main__":
    build_program()
    print("build ok")


# version 13
# speedup vs baseline: 1.1039x; 1.1039x over previous
import numpy as np
import ml_dtypes
import concourse.bass as bass
import concourse.mybir as mybir
from concourse.bass_utils import run_bass_kernel_spmd
from concourse.aot_env import allow_ici_all_to_all
from contextlib import ExitStack

F32 = mybir.dt.float32
BF16 = mybir.dt.bfloat16
F8 = mybir.dt.float8e4
I32 = mybir.dt.int32
AF = mybir.ActivationFunctionType
ALU = mybir.AluOpType

NCORE = 8
D = 1024
DL = 128
BB = 4
L = 4096
TOKS = BB * L
TL = 2048
NT = TL // 128
T = 128
CH = 32
U = BB * CH
NM = 32
NE = 129
GEN = NE * NM
GW = 4160

PI = float(np.pi)
TWO_PI = 2.0 * PI
INV2PI = 1.0 / TWO_PI

NGRP = 8
GC = 16
NB = 32


def build_program():
    nc = bass.Bass("TRN2", target_bir_lowering=False)

    xs = nc.dram_tensor("xs", [TL, D], F32, kind="ExternalInput")
    ldt = nc.dram_tensor("ldt", [DL, 1], F32, kind="ExternalInput")
    lar = nc.dram_tensor("lar", [DL, NM], F32, kind="ExternalInput")
    aim = nc.dram_tensor("aim", [DL, NM], F32, kind="ExternalInput")
    cre = nc.dram_tensor("cre", [DL, NM], F32, kind="ExternalInput")
    cim = nc.dram_tensor("cim", [DL, NM], F32, kind="ExternalInput")
    WTd = nc.dram_tensor("WT", [D, D], F32, kind="ExternalInput")
    WFTd = nc.dram_tensor("WFT", [D, D], F32, kind="ExternalInput")
    battn_r = nc.dram_tensor("battn_r", [1, D], F32, kind="ExternalInput")
    bffn_r = nc.dram_tensor("bffn_r", [1, D], F32, kind="ExternalInput")
    isc = nc.dram_tensor("isc", [DL, 1], F32, kind="ExternalInput")
    wsc = nc.dram_tensor("wsc", [DL, 1], F32, kind="ExternalInput")
    iotaE = nc.dram_tensor("iotaE", [DL, NE], F32, kind="ExternalInput")
    jmat = nc.dram_tensor("jmat", [128, 128], BF16, kind="ExternalInput")
    ones1 = nc.dram_tensor("ones1", [1, 128], BF16, kind="ExternalInput")
    out_t = nc.dram_tensor("out", [TL, D], F32, kind="ExternalOutput")

    a1_in = nc.dram_tensor("a1_in", [NCORE, TL, DL], BF16)
    a1_out = nc.dram_tensor("a1_out", [TOKS, DL], BF16)
    y_scr = nc.dram_tensor("y_scr", [DL, TOKS], BF16)
    a2_in = nc.dram_tensor("a2_in", [NCORE, DL, TL], BF16)
    a2_out = nc.dram_tensor("a2_out", [D, TL], BF16)
    xq_scr = nc.dram_tensor("xq_scr", [TL, D], BF16)
    g_scr = nc.dram_tensor("g_scr", [TL, D], F32)
    kpadR = nc.dram_tensor("kpadR", [DL, 256], BF16)
    waT = nc.dram_tensor("waT", [T, DL, 64], BF16)
    eT1 = nc.dram_tensor("eT1", [DL, T, 128], BF16)
    rTre = nc.dram_tensor("rTre", [128, DL], F32)
    rTim = nc.dram_tensor("rTim", [128, DL], F32)
    uReD = nc.dram_tensor("uReD", [128, DL, CH], F32)
    uImD = nc.dram_tensor("uImD", [128, DL, CH], F32)
    s_scr = nc.dram_tensor("s_scr", [DL, 64, U], BF16)

    ctx = ExitStack()
    ST = lambda *a, **k: ctx.enter_context(nc.sbuf_tensor(*a, **k))
    PT = lambda *a: ctx.enter_context(nc.psum_tensor(*a))

    c_iota = ST("c_iota", [DL, NE], F32)
    c_j = ST("c_j", [128, 128], BF16)
    c_ones1 = ST("c_ones1", [1, 128], BF16)
    p_ldt = ST("p_ldt", [DL, 1], F32)
    p_lar = ST("p_lar", [DL, NM], F32)
    p_aim = ST("p_aim", [DL, NM], F32)
    p_cre = ST("p_cre", [DL, NM], F32)
    p_cim = ST("p_cim", [DL, NM], F32)
    p_isc = ST("p_isc", [DL, 1], F32)
    p_wsc = ST("p_wsc", [DL, 1], F32)
    v_invs = ST("v_invs", [DL, 1], F32)
    v_s2 = ST("v_s2", [DL, 1], F32)
    v_rs2 = ST("v_rs2", [DL, 1], F32)
    d_dt = ST("d_dt", [DL, 1], F32)
    d_elar = ST("d_elar", [DL, NM], F32)
    d_lre = ST("d_lre", [DL, NM], F32)
    d_lim = ST("d_lim", [DL, NM], F32)
    d_ia2 = ST("d_ia2", [DL, NM], F32)
    d_t1 = ST("d_t1", [DL, NM], F32)
    d_t2 = ST("d_t2", [DL, NM], F32)
    d_t3 = ST("d_t3", [DL, NM], F32)
    d_wre = ST("d_wre", [DL, NM], F32)
    d_wim = ST("d_wim", [DL, NM], F32)
    d_cfr = ST("d_cfr", [DL, NM], F32)
    d_cfi = ST("d_cfi", [DL, NM], F32)
    d_aimn = ST("d_aimn", [DL, NM], F32)
    t_K = ST("t_K", [DL, NE], F32)
    z128 = ST("z128", [DL, 128], BF16)

    gA = ST("gA", [128, GW], F32)
    gB = ST("gB", [128, GW], F32)
    gC = ST("gC", [128, GW], F32)
    gI = ST("gI", [128, GW], I32)
    m_are = ST("m_are", [128, DL], F32)
    m_aim = ST("m_aim", [128, DL], F32)
    w_t1 = ST("w_t1", [128, 128], F32)
    w_t2 = ST("w_t2", [128, 128], F32)
    w_t3 = ST("w_t3", [128, 128], F32)
    w_t4 = ST("w_t4", [128, 128], F32)

    x_tl = [gA[:, 0:1024], gA[:, 1024:2048], gA[:, 2048:3072]]
    xn_tl = [gB[:, 0:1024], gB[:, 1024:2048]]
    xq1_tl = [gB[:, 2048:3072], gB[:, 3072:4096]]
    gIf = gI[:, 0:GEN].bitcast(F32)

    def u_sb_ap(grp, off, dims):
        base = (grp % 2) * 64 * GW
        d2 = [list(d) for d in dims]
        return bass.AP(gI, base + off, d2).bitcast(F32)
    s_sb = [gA[:, 0:1024], gA[:, 1024:2048], gA[:, 2048:3072], gA[:, 3072:4096]]
    gg_sb = [gB[:, 0:1024], gB[:, 1024:2048], gB[:, 2048:3072], gB[:, 3072:4096]]
    xr_tl = [gC[:, 0:1024], gC[:, 1024:2048]]
    o_sb = [gC[:, 2048:3072], gC[:, 3072:4096]]
    gtmp = [gC[:, 2048:3072], gC[:, 3072:4096]]

    s_bn = ST("s_bn", [128, 12], F32)
    s_mv1 = ST("s_mv1", [128, 2], F32)
    s_mv2 = ST("s_mv2", [128, 2], F32)
    s_mean = ST("s_mean", [128, 1], F32)
    s_tmp = ST("s_tmp", [128, 1], F32)
    s_var = ST("s_var", [128, 1], F32)
    s_rstd = [ST("s_rstd0", [128, 1], F32), ST("s_rstd1", [128, 1], F32)]
    s_dmy = ST("s_dmy", [1, 1], F32)
    t_KB = ST("t_KB", [DL, NE], BF16)
    xnb_tl = [ST("xnb0", [128, D], BF16), ST("xnb1", [128, D], BF16)]
    powb = ST("powb", [128, 2 * GEN], BF16)
    powCb = powb[:, 0:GEN]
    powSb = powb[:, GEN:2 * GEN]
    waReB = powb[:, 0:GEN]
    waImB = powb[:, GEN:2 * GEN]
    scbRe = powb[:, 0:DL * CH]
    scbIm = powb[:, GEN:GEN + DL * CH]
    f8t = [ST("f8t0", [128, D], F8)] * 2
    xqbt = [powb[:, 0:D], powb[:, D:2 * D]]
    s_nmr = [ST("s_nmr0", [128, 1], F32), ST("s_nmr1", [128, 1], F32)]

    wft_b = ST("wft_b", [128, 8 * D], BF16)
    wt_b = wft_b
    bffn_row = ST("bffn_row", [1, D], BF16)
    bffn_sb = ST("bffn_sb", [1, D], BF16)
    battn_bb = ST("battn_bb", [1, D], BF16)
    xdqT = ST("xdqT", [128, 8 * TL], BF16, is_transpose_dest=True)

    x_all = ST("x_all", [128, U * 128], BF16)
    wa_b = [ST("wa_b0", [128, GC * 64], BF16), ST("wa_b1", [128, GC * 64], BF16)]
    hk_b = [ST("hk_b0", [128, GC * 128], BF16)] * 2
    e_b = [ST("e_b0", [128, GC * 128], BF16, is_transpose_dest=True)] * 2
    sm_b = [ST("sm_b0", [64, GC * U], BF16)] * 2
    yg_sb = [ST("yg_sb0", [128, 512], BF16), ST("yg_sb1", [128, 512], BF16)]
    yt_sb = [ST("yt_sb0", [128, 512], BF16), ST("yt_sb1", [128, 512], BF16)]
    sst_tl = [ST(f"sst{i}", [128, 8 * 128], BF16) for i in range(4)]

    psU = [PT("psU0", [64, 512], F32), PT("psU1", [64, 512], F32)]
    psUB = None
    psY = [PT("psY0", [128, 512], F32), PT("psY1", [128, 512], F32)]
    psT = [PT("psT0", [128, 512], BF16), PT("psT1", [128, 512], BF16)]
    psF = [PT("psF0", [128, 512], F32), PT("psF1", [128, 512], F32)]
    psUB = [psU[0], psU[1], psF[0], psF[1]]

    Q1 = ctx.enter_context(nc.semaphore("Q1"))
    XT = [ctx.enter_context(nc.semaphore(f"XT{i}")) for i in range(3)]
    SDX = [ctx.enter_context(nc.semaphore(f"SDX{i}")) for i in range(2)]
    SU = [ctx.enter_context(nc.semaphore(f"SU{i}")) for i in range(2)]
    GWS = [ctx.enter_context(nc.semaphore(f"GWS{i}")) for i in range(2)]
    SYP = [ctx.enter_context(nc.semaphore(f"SYP{i}")) for i in range(2)]
    XW = [ctx.enter_context(nc.semaphore(f"XW{i}")) for i in range(2)]
    GS = ctx.enter_context(nc.semaphore("GS"))
    QP = [ctx.enter_context(nc.semaphore(f"QP{i}")) for i in range(4)]
    QW = [ctx.enter_context(nc.semaphore(f"QW{i}")) for i in range(2)]
    GPD = ctx.enter_context(nc.semaphore("GPD"))
    GPC = ctx.enter_context(nc.semaphore("GPC"))
    VS = ctx.enter_context(nc.semaphore("VS"))
    SS = ctx.enter_context(nc.semaphore("SS"))
    VX = ctx.enter_context(nc.semaphore("VX"))
    SX = ctx.enter_context(nc.semaphore("SX"))
    TS = ctx.enter_context(nc.semaphore("TS"))

    M = {}
    q = 10
    M["q1_params"] = q * 16
    for i in range(NT):
        M[f"xt_{i}"] = 16 * (i // 3 + 1)
    q += 1; M["q1_zs"] = q * 16
    q += 8; M["q1_tr15"] = q * 16
    q += 16; M["q1_xall"] = q * 16
    for g in range(NGRP):
        M[f"qw_{g}"] = 16 * (g // 2 + 1)
    q += 2; M["q1_mult"] = q * 16
    q += 2; M["q1_scanrd"] = q * 16
    q += 4; M["q1_sscrw"] = q * 16
    for g in range(NGRP):
        q += 1; M[f"q1_hk{g}"] = q * 16
        q += 2; M[f"q1_e{g}"] = q * 16
        q += 1; M[f"q1_sm{g}"] = q * 16
    for i in range(NT):
        M[f"qp_{i}"] = 48 * (i // 4 + 1)
    M["q1_final"] = q * 16

    for i in range(NT):
        M[f"sdx_{i}"] = 16 * (i // 2 + 1)
        M[f"gw_{i}"] = 16 * (i // 2 + 1)
    for g in range(NGRP):
        M[f"su_{g}"] = 32 * (g // 2 + 1)
    for k in range(NB):
        M[f"sy_{k}"] = 16 * (k // 2 + 1)
    for i in range(NT):
        M[f"xw_{i}"] = 16 * (i // 2 + 1)
    g = 0
    g += 2; M["gs_eT"] = g * 16
    g += 2; M["gs_waTim"] = g * 16
    g += 2; M["gs_kpad"] = g * 16
    g += 8; M["gs_rT"] = g * 16


    p = 0
    p += 9; M["gpd_wft"] = p * 16
    p += 1; M["gpd_yshuf"] = p * 16
    p += 8; M["gpd_wt"] = p * 16
    p += 1; M["gpd_battn"] = p * 16
    for i in range(NT):
        p += 1; M[f"gpd_out{i}"] = p * 16
    M["gpd_final"] = p * 16

    M["gpc_cc1"] = 1
    M["gpc_cc2"] = 2
    for i in range(NT):
        M[f"gpc_comb{i}"] = 2 + 2 * (i + 1)

    s = 0
    s += 1; M["vs_small0"] = s
    for i in range(NT):
        s += 1; M[f"vs_rv{i}"] = s
        if i >= 1:
            s += 1; M[f"vs_clip{i-1}"] = s
            s += 1; M[f"vs_xnb{i-1}"] = s
    s += 1; M[f"vs_clip{NT-1}"] = s
    s += 1; M[f"vs_xnb{NT-1}"] = s
    s += 1; M["vs_bffn"] = s
    s += 1; M["vs_lam"] = s
    s += 1; M["vs_k1"] = s
    s += 1; M["vs_red1"] = s
    s += 1; M["vs_k2"] = s
    s += 1; M["vs_red2"] = s
    s += 1; M["vs_decay"] = s
    s += 1; M["vs_pow"] = s
    s += 1; M["vs_powb"] = s
    s += 1; M["vs_ceff"] = s
    s += 1; M["vs_waRe"] = s
    s += 1; M["vs_waIm"] = s
    s += 1; M["vs_kred"] = s
    s += 1; M["vs_tkb"] = s
    s += 1; M["vs_rtneg"] = s
    for k in range(NB):
        s += 1; M[f"vs_uev{k}"] = s
    s += 1; M["vs_scan"] = s
    s += 1; M["vs_scastb"] = s
    for k in range(NB):
        s += 1; M[f"vs_yt{k}"] = s
    M["vs_final"] = s

    s = 0
    s += 1; M["ss_gen0"] = s
    for i in range(NT):
        s += 1; M[f"ss_aff{i}"] = s
        if i >= 1:
            s += 1; M[f"ss_xq{i-1}"] = s
    s += 1; M[f"ss_xq{NT-1}"] = s
    s += 1; M["ss_int1"] = s
    s += 1; M["ss_sin"] = s
    s += 1; M["ss_int2"] = s
    s += 1; M["ss_cos"] = s
    s += 1; M["ss_exp"] = s
    for i in range(NT):
        s += 1; M[f"ss_fev{i}"] = s
    for k in range(NB):
        s += 1; M[f"ss_gel{k}"] = s
    for i in range(NT):
        s += 1; M[f"ss_sev{i}"] = s
    M["ss_final"] = s

    s = 0
    for i in range(NT):
        s += 1; M[f"ts_ffn{i}"] = s
    for k in range(NB):
        s += 1; M[f"ts_2a{k}"] = s
    for k in range(NB):
        s += 1; M[f"ts_2cm{k}"] = s
        s += 1; M[f"ts_2ct{k}"] = s
    for i in range(NT):
        s += 1; M[f"ts_p3_{i}"] = s
    M["ts_final"] = s

    with nc.Block() as block:

        @block.sync
        def _(sync):
            sync.dma_start(out=c_iota[:, :], in_=iotaE[:, :]).then_inc(Q1, 16)
            sync.dma_start(out=c_j[:, :], in_=jmat[:, :]).then_inc(Q1, 16)
            sync.dma_start(out=c_ones1[:, :], in_=ones1[:, :]).then_inc(Q1, 16)
            sync.dma_start(out=p_ldt[:, :], in_=ldt[:, :]).then_inc(Q1, 16)
            sync.dma_start(out=p_lar[:, :], in_=lar[:, :]).then_inc(Q1, 16)
            sync.dma_start(out=p_aim[:, :], in_=aim[:, :]).then_inc(Q1, 16)
            sync.dma_start(out=p_cre[:, :], in_=cre[:, :]).then_inc(Q1, 16)
            sync.dma_start(out=p_cim[:, :], in_=cim[:, :]).then_inc(Q1, 16)
            sync.dma_start(out=p_isc[:, :], in_=isc[:, :]).then_inc(Q1, 16)
            sync.dma_start(out=p_wsc[:, :], in_=wsc[:, :]).then_inc(Q1, 16)

            for i in range(NT):
                if i >= 3:
                    sync.wait_ge(SS, M[f"ss_aff{i-3}"])
                sync.dma_start(out=x_tl[i % 3],
                               in_=xs[i * 128:(i + 1) * 128, :]).then_inc(XT[i % 3], 16)
                if i >= 1:
                    j = i - 1
                    sync.wait_ge(VS, M[f"vs_xnb{j}"])
                    dst = bass.AP(a1_in, j * 128 * DL,
                                  [[DL, 128], [TL * DL, NCORE], [1, DL]])
                    sync.dma_start(out=dst, in_=xnb_tl[j % 2][:, :]).then_inc(XW[j % 2], 16)
            j = NT - 1
            sync.wait_ge(VS, M[f"vs_xnb{j}"])
            dst = bass.AP(a1_in, j * 128 * DL, [[DL, 128], [TL * DL, NCORE], [1, DL]])
            sync.dma_start(out=dst, in_=xnb_tl[j % 2][:, :]).then_inc(XW[j % 2], 16)

            sync.wait_ge(VS, M["vs_small0"])
            zsrc = bass.AP(z128, 0, [[128, 128], [0, 64], [1, 128]])
            zdst = bass.AP(s_scr, 0, [[64 * U, 128], [128, 64], [1, 128]])
            sync.dma_start(out=zdst, in_=zsrc).then_inc(Q1, 16)
            sync.wait_ge(Q1, M["q1_params"])
            sync.wait_ge(SDX[0], M[f"sdx_{NT-2}"])
            sync.wait_ge(SDX[1], M[f"sdx_{NT-1}"])
            for k in range(8):
                sync.dma_start(out=xdqT[:, k * TL:(k + 1) * TL],
                               in_=xq_scr[:, k * 128:(k + 1) * 128],
                               transpose=True).then_inc(Q1, 16)

            sync.wait_ge(Q1, M["q1_tr15"])
            sync.wait_ge(GPC, M["gpc_cc1"])
            for j in range(16):
                src = bass.AP(a1_out, j * 8 * 128 * DL,
                              [[DL, 128], [128 * DL, 8], [1, DL]])
                sync.dma_start(out=x_all[:, j * 1024:(j + 1) * 1024],
                               in_=src).then_inc(Q1, 16)
            sync.wait_ge(GS, M["gs_waTim"])
            for g in range(NGRP):
                if g >= 2:
                    sync.wait_ge(TS, M[f"ts_2a{4*(g-1)-1}"])
                src = bass.AP(waT, g * GC * 64, [[DL * 64, T], [1, GC * 64]])
                sync.dma_start(out=wa_b[g % 2][:, :], in_=src).then_inc(QW[g % 2], 16)

            sync.wait_ge(GS, M["gs_rT"])
            sync.dma_start(out=m_are[:, :], in_=rTre[:, :]).then_inc(Q1, 16)
            sync.dma_start(out=m_aim[:, :], in_=rTim[:, :]).then_inc(Q1, 16)
            sync.wait_ge(Q1, M["q1_mult"])
            sync.wait_ge(SU[0], M[f"su_{NGRP-2}"])
            sync.wait_ge(SU[1], M[f"su_{NGRP-1}"])
            sync.wait_ge(VS, M["vs_rtneg"])
            for srcT, dstT in ((uReD, gA), (uImD, gB)):
                srcp = bass.AP(srcT, 0, [[DL * CH, 128], [1, DL * CH]])
                dstp = bass.AP(dstT, 0, [[GW, 128], [1, DL * CH]])
                sync.dma_start(out=dstp, in_=srcp).then_inc(Q1, 16)

            sync.wait_ge(VS, M["vs_scastb"])
            for comp, srcoff in ((0, 0), (32, GEN)):
                for dh in range(2):
                    srcp = bass.AP(powb, srcoff + dh * 64 * 32,
                                   [[2 * GEN, 128], [32, 64], [1, 31]])
                    dstp = bass.AP(s_scr, comp * U + 1 + dh * 64 * 64 * U,
                                   [[32, 128], [64 * U, 64], [1, 31]])
                    sync.dma_start(out=dstp, in_=srcp).then_inc(Q1, 16)
            sync.wait_ge(Q1, M["q1_sscrw"])
            sync.wait_ge(TS, M[f"ts_2a{NB-1}"])
            sync.wait_ge(GS, M["gs_kpad"])
            for g in range(NGRP):
                if g >= 1:
                    sync.wait_ge(TS, M[f"ts_2cm{4*g-1}"])
                src = bass.AP(kpadR, g * GC * 256, [[1, 128], [256, GC], [1, 128]])
                sync.dma_start(out=hk_b[g % 2][:, :], in_=src).then_inc(Q1, 16)
                for kk in range(2):
                    src = bass.AP(eT1, (g * 2 + kk) * 8 * T * 128,
                                  [[128, 8 * T], [1, 128]])
                    sync.dma_start(out=e_b[g % 2][:, kk * 1024:(kk + 1) * 1024],
                                   in_=src, transpose=True).then_inc(Q1, 16)
                src = bass.AP(s_scr, g * GC * 64 * U, [[U, 64], [64 * U, GC], [1, U]])
                sync.dma_start(out=sm_b[g % 2][:, :], in_=src).then_inc(Q1, 16)

            sync.wait_ge(GPC, M["gpc_cc2"])
            for i in range(NT):
                if i >= 4:
                    sync.wait_ge(TS, M[f"ts_p3_{i-4}"])
                    sync.wait_ge(GPC, M[f"gpc_comb{i-4}"])
                elif i >= 2:
                    pass
                src = bass.AP(a2_out, i * 128, [[TL, 128], [128 * TL, 8], [1, 128]])
                sync.dma_start(out=sst_tl[i % 4][:, :], in_=src).then_inc(QP[i % 4], 16)
                if i >= 2:
                    sync.wait_ge(GPC, M[f"gpc_comb{i-2}"])
                sync.dma_start(out=xr_tl[i % 2],
                               in_=xs[i * 128:(i + 1) * 128, :]).then_inc(QP[i % 4], 16)
                sync.wait_ge(GWS[i % 2], M[f"gw_{i}"])
                sync.dma_start(out=gg_sb[i % 4],
                               in_=g_scr[i * 128:(i + 1) * 128, :]).then_inc(QP[i % 4], 16)

        @block.gpsimd
        def _(gpsimd):
            for k in range(8):
                gpsimd.dma_start(out=wft_b[:, k * D:(k + 1) * D],
                                 in_=WFTd[k * 128:(k + 1) * 128, :]).then_inc(GPD, 16)
            gpsimd.dma_start(out=bffn_row[:, :], in_=bffn_r[:, :]).then_inc(GPD, 16)
            gpsimd.wait_ge(GPD, M["gpd_wft"])
            gpsimd.wait_ge(XW[0], M[f"xw_{NT-2}"])
            gpsimd.wait_ge(XW[1], M[f"xw_{NT-1}"])
            with allow_ici_all_to_all():
                gpsimd.collective_compute(
                    "AllToAll", ALU.bypass,
                    replica_groups=[list(range(NCORE))],
                    ins=[a1_in.ap().opt()],
                    outs=[a1_out.ap().opt()],
                ).then_inc(GPC, 1)
            gpsimd.wait_ge(SYP[0], M[f"sy_{NB-2}"])
            gpsimd.wait_ge(SYP[1], M[f"sy_{NB-1}"])
            src = bass.AP(y_scr, 0, [[TL, NCORE], [TOKS, DL], [1, TL]])
            dst = bass.AP(a2_in, 0, [[DL * TL, NCORE], [TL, DL], [1, TL]])
            gpsimd.dma_start(out=dst, in_=src).then_inc(GPD, 16)
            gpsimd.wait_ge(GPD, M["gpd_yshuf"])
            gpsimd.wait_ge(SS, M[f"ss_fev{NT-1}"])
            for k in range(8):
                gpsimd.dma_start(out=wt_b[:, k * D:(k + 1) * D],
                                 in_=WTd[k * 128:(k + 1) * 128, :]).then_inc(GPD, 16)
            gpsimd.dma_start(out=battn_bb[:, :], in_=battn_r[:, :]).then_inc(GPD, 16)
            with allow_ici_all_to_all():
                gpsimd.collective_compute(
                    "AllToAll", ALU.bypass,
                    replica_groups=[list(range(NCORE))],
                    ins=[a2_in.ap().opt()],
                    outs=[a2_out.ap().opt()],
                ).then_inc(GPC, 1)
            for i in range(NT):
                gpsimd.wait_ge(SS, M[f"ss_sev{i}"])
                gpsimd.wait_ge(QP[i % 4], M[f"qp_{i}"])
                if i >= 2:
                    gpsimd.wait_ge(GPD, M[f"gpd_out{i-2}"])
                gpsimd.tensor_tensor(out=o_sb[i % 2], in0=s_sb[i % 4],
                                     in1=gg_sb[i % 4], op=ALU.mult).then_inc(GPC, 1)
                gpsimd.tensor_tensor(out=o_sb[i % 2], in0=o_sb[i % 2],
                                     in1=xr_tl[i % 2], op=ALU.add).then_inc(GPC, 1)
                gpsimd.dma_start(out=out_t[i * 128:(i + 1) * 128, :],
                                 in_=o_sb[i % 2]).then_inc(GPD, 16)
            gpsimd.wait_ge(GPD, M["gpd_final"])

        @block.vector
        def _(vector):
            vxc = [0]

            def vg(instr):
                vxc[0] += 1
                instr.then_inc(VX, 1)
                vector.wait_ge(VX, vxc[0])
                return instr

            vector.wait_ge(Q1, M["q1_params"])
            vector.reciprocal(out=v_invs[:, :], in_=p_isc[:, :])
            vg(vector.tensor_mul(v_s2[:, :], p_isc[:, :], p_wsc[:, :]))
            vg(vector.reciprocal(out=v_rs2[:, :], in_=v_s2[:, :]))
            vector.memset(z128[:, :], 0.0).then_inc(VS, 1)

            def emit_clip(j):
                vector.wait_ge(SS, M[f"ss_aff{j}"])
                vg(vector.tensor_scalar(out=xq1_tl[j % 2], in0=xn_tl[j % 2],
                                        scalar1=v_invs[:, 0:1], scalar2=448.0,
                                        op0=ALU.mult, op1=ALU.min))
                vector.tensor_scalar(out=xq1_tl[j % 2], in0=xq1_tl[j % 2],
                                     scalar1=-448.0, scalar2=0.5,
                                     op0=ALU.max, op1=ALU.mult).then_inc(VS, 1)
                if j >= 2:
                    vector.wait_ge(XW[j % 2], M[f"xw_{j-2}"])
                vector.tensor_copy(out=xnb_tl[j % 2][:, :],
                                   in_=xn_tl[j % 2]).then_inc(VS, 1)

            for i in range(NT):
                vector.wait_ge(XT[i % 3], M[f"xt_{i}"])
                xt = x_tl[i % 3]
                vector.bn_stats(out=s_bn[:, 0:6], in_=xt[:, 0:512])
                vg(vector.bn_stats(out=s_bn[:, 6:12], in_=xt[:, 512:1024]))
                vector.bn_aggr(out=s_mv1[:, :], in_=s_bn[:, 0:6])
                vg(vector.bn_aggr(out=s_mv2[:, :], in_=s_bn[:, 6:12]))
                vector.tensor_sub(s_tmp[:, :], s_mv1[:, 0:1], s_mv2[:, 0:1])
                vg(vector.tensor_add(s_mv1[:, :], s_mv1[:, :], s_mv2[:, :]))
                vector.tensor_mul(s_tmp[:, :], s_tmp[:, :], s_tmp[:, :])
                vg(vector.tensor_scalar_mul(out=s_mv1[:, :], in0=s_mv1[:, :],
                                            scalar1=0.5))
                vg(vector.tensor_scalar_mul(out=s_nmr[i % 2][:, :],
                                            in0=s_mv1[:, 0:1], scalar1=-1.0))
                vg(vector.scalar_tensor_tensor(out=s_var[:, :], in0=s_tmp[:, :],
                                               scalar=0.25, in1=s_mv1[:, 1:2],
                                               op0=ALU.mult, op1=ALU.add))
                vg(vector.tensor_scalar_add(out=s_var[:, :], in0=s_var[:, :],
                                            scalar1=1e-5))
                vector.reciprocal(out=s_rstd[i % 2][:, :],
                                  in_=s_var[:, :]).then_inc(VS, 1)
                if i >= 1:
                    emit_clip(i - 1)
            emit_clip(NT - 1)

            vector.wait_ge(GPD, M["gpd_wft"])
            vector.tensor_scalar_mul(out=bffn_sb[:, :], in0=bffn_row[:, :],
                                     scalar1=v_rs2[0:1, 0:1]).then_inc(VS, 1)
            vector.wait_ge(SS, M["ss_gen0"])
            vector.tensor_scalar(out=d_lre[:, :], in0=d_elar[:, :],
                                 scalar1=d_dt[:, 0:1], scalar2=-1.0,
                                 op0=ALU.mult, op1=ALU.mult)
            vector.tensor_scalar(out=d_lim[:, :], in0=p_aim[:, :],
                                 scalar1=d_dt[:, 0:1], scalar2=None, op0=ALU.mult)
            vector.tensor_mul(d_t1[:, :], d_elar[:, :], d_elar[:, :])
            vg(vector.tensor_mul(d_t2[:, :], p_aim[:, :], p_aim[:, :]))
            vg(vector.tensor_add(d_t1[:, :], d_t1[:, :], d_t2[:, :]))
            vector.reciprocal(out=d_ia2[:, :], in_=d_t1[:, :]).then_inc(VS, 1)
            gAv = gA[:, 0:GEN]
            gBv = gB[:, 0:GEN]
            gCv = gC[:, 0:GEN]
            lim_bc = bass.AP(d_lim, 0, [[NM, 128], [0, NE], [1, NM]])
            lre_bc = bass.AP(d_lre, 0, [[NM, 128], [0, NE], [1, NM]])
            iot_bc = bass.AP(c_iota, 0, [[NE, 128], [1, NE], [0, NM]])
            vector.tensor_tensor(out=gAv, in0=iot_bc, in1=lim_bc, op=ALU.mult)
            vector.wait_ge(SS, M[f"ss_xq{NT-1}"])
            vector.tensor_scalar(out=gBv, in0=gAv, scalar1=INV2PI, scalar2=None,
                                 op0=ALU.mult).then_inc(VS, 1)
            vector.wait_ge(SS, M["ss_int1"])
            vector.scalar_tensor_tensor(out=gBv, in0=gBv, scalar=-TWO_PI,
                                        in1=gAv, op0=ALU.mult,
                                        op1=ALU.add).then_inc(VS, 1)
            vector.wait_ge(SDX[0], M[f"sdx_{NT-2}"])
            vector.wait_ge(SDX[1], M[f"sdx_{NT-1}"])
            vector.tensor_scalar_add(out=gCv, in0=gAv, scalar1=PI / 2.0)
            vector.tensor_scalar(out=gAv, in0=gCv, scalar1=INV2PI, scalar2=None,
                                 op0=ALU.mult).then_inc(VS, 1)
            vector.wait_ge(SS, M["ss_int2"])
            vector.scalar_tensor_tensor(out=gCv, in0=gAv, scalar=-TWO_PI,
                                        in1=gCv, op0=ALU.mult,
                                        op1=ALU.add).then_inc(VS, 1)
            vector.wait_ge(SS, M["ss_cos"])
            vector.tensor_tensor(out=gAv, in0=iot_bc, in1=lre_bc,
                                 op=ALU.mult).then_inc(VS, 1)
            vector.wait_ge(SS, M["ss_exp"])
            vector.tensor_mul(gCv, gAv, gCv)
            vector.tensor_mul(gBv, gAv, gBv).then_inc(VS, 1)
            vector.tensor_copy(out=bass.AP(powb, 0, [[2 * GEN, 128], [64, NE], [1, NM]]),
                               in_=gCv)
            vector.tensor_copy(out=bass.AP(powb, NM, [[2 * GEN, 128], [64, NE], [1, NM]]),
                               in_=gBv).then_inc(VS, 1)
            sl = slice(127 * NM, 128 * NM)
            vector.tensor_scalar_add(out=d_t1[:, :], in0=gC[:, sl], scalar1=-1.0)
            vg(vector.tensor_copy(out=d_t2[:, :], in_=gB[:, sl]))
            vector.tensor_mul(d_wre[:, :], d_t1[:, :], d_elar[:, :])
            vg(vector.tensor_mul(d_t3[:, :], d_t2[:, :], p_aim[:, :]))
            vg(vector.tensor_sub(d_wre[:, :], d_t3[:, :], d_wre[:, :]))
            vg(vector.tensor_mul(d_wre[:, :], d_wre[:, :], d_ia2[:, :]))
            vector.tensor_mul(d_wim[:, :], d_t2[:, :], d_elar[:, :])
            vg(vector.tensor_mul(d_t3[:, :], d_t1[:, :], p_aim[:, :]))
            vg(vector.tensor_add(d_wim[:, :], d_wim[:, :], d_t3[:, :]))
            vg(vector.tensor_scalar_mul(out=d_wim[:, :], in0=d_wim[:, :],
                                        scalar1=-1.0))
            vg(vector.tensor_mul(d_wim[:, :], d_wim[:, :], d_ia2[:, :]))
            vector.tensor_mul(d_cfr[:, :], p_cre[:, :], d_wre[:, :])
            vg(vector.tensor_mul(d_t3[:, :], p_cim[:, :], d_wim[:, :]))
            vg(vector.tensor_sub(d_cfr[:, :], d_cfr[:, :], d_t3[:, :]))
            vector.tensor_mul(d_cfi[:, :], p_cre[:, :], d_wim[:, :])
            vg(vector.tensor_mul(d_t3[:, :], p_cim[:, :], d_wre[:, :]))
            vg(vector.tensor_add(d_cfi[:, :], d_cfi[:, :], d_t3[:, :]))
            vector.engine_nop().then_inc(VS, 1)
            cfr_bc = bass.AP(d_cfr, 0, [[NM, 128], [0, 128], [1, NM]])
            cfi_bc = bass.AP(d_cfi, 0, [[NM, 128], [0, 128], [1, NM]])
            A32 = bass.AP(gA, NM, [[GW, 128], [1, 128 * NM]])
            B32 = bass.AP(gB, NM, [[GW, 128], [1, 128 * NM]])
            C32 = bass.AP(gC, NM, [[GW, 128], [1, 128 * NM]])
            I32 = bass.AP(gI, 0, [[GW, 128], [1, 128 * NM]]).bitcast(F32)
            waRe_o = bass.AP(powb, 0, [[2 * GEN, 128], [64, 128], [1, NM]])
            waIm_o = bass.AP(powb, NM, [[2 * GEN, 128], [64, 128], [1, NM]])
            vector.wait_ge(GS, M["gs_eT"])
            vector.scalar_tensor_tensor(out=A32, in0=C32, scalar=2.0,
                                        in1=cfr_bc, op0=ALU.mult, op1=ALU.mult)
            vector.scalar_tensor_tensor(out=C32, in0=C32, scalar=-2.0,
                                        in1=cfi_bc, op0=ALU.mult, op1=ALU.mult)
            vg(vector.scalar_tensor_tensor(out=I32, in0=B32, scalar=2.0,
                                           in1=cfi_bc, op0=ALU.mult, op1=ALU.mult))
            vector.tensor_tensor(out=waRe_o, in0=A32, in1=I32,
                                 op=ALU.subtract).then_inc(VS, 1)
            vg(vector.scalar_tensor_tensor(out=I32, in0=B32, scalar=-2.0,
                                           in1=cfr_bc, op0=ALU.mult, op1=ALU.mult))
            vector.tensor_tensor(out=waIm_o, in0=I32, in1=C32,
                                 op=ALU.add).then_inc(VS, 1)
            vector.tensor_reduce(out=t_K[:, 1:NE],
                                 in_=bass.AP(powb, 0, [[2 * GEN, 128], [64, 128], [1, 32]]),
                                 axis=mybir.AxisListType.X,
                                 op=ALU.add).then_inc(VS, 1)
            vector.tensor_copy(out=t_KB[:, :], in_=t_K[:, :]).then_inc(VS, 1)
            vector.tensor_scalar_mul(out=d_aimn[:, :], in0=gB[:, 0:NM],
                                     scalar1=-1.0).then_inc(VS, 1)

            for k in range(NB):
                g = k // 4
                vector.wait_ge(TS, M[f"ts_2a{k}"])
                chb = (k % 4) * 4
                src = bass.AP(psUB[k % 4], 0, [[512, 64], [128, 4], [32, 4], [1, 32]])
                dst = u_sb_ap(g, chb * CH,
                              [[GW, 64], [CH, 4], [GC * CH, 4], [1, CH]])
                vector.tensor_copy(out=dst, in_=src).then_inc(VS, 1)
                if k % 4 == 3 and g >= 1 and g < NGRP - 1:
                    vector.wait_ge(SU[g % 2], M[f"su_{g-1}"])

            vector.wait_ge(Q1, M["q1_scanrd"])
            last = None
            for c in range(1, CH):
                sre_p = bass.AP(gA, c - 1, [[GW, 128], [32, DL]])
                sim_p = bass.AP(gB, c - 1, [[GW, 128], [32, DL]])
                sre_n = bass.AP(gA, c, [[GW, 128], [32, DL]])
                sim_n = bass.AP(gB, c, [[GW, 128], [32, DL]])
                vector.tensor_mul(w_t1[:, :], m_are[:, :], sre_p)
                vector.tensor_mul(w_t2[:, :], m_aim[:, :], sim_p)
                vector.tensor_mul(w_t3[:, :], m_aim[:, :], sre_p)
                vg(vector.tensor_mul(w_t4[:, :], m_are[:, :], sim_p))
                vector.tensor_sub(w_t1[:, :], w_t1[:, :], w_t2[:, :])
                vg(vector.tensor_add(w_t3[:, :], w_t3[:, :], w_t4[:, :]))
                vector.tensor_add(sre_n, sre_n, w_t1[:, :])
                last = vector.tensor_add(sim_n, sim_n, w_t3[:, :])
                vg(last)
            vector.engine_nop().then_inc(VS, 1)
            vector.wait_ge(GS, M["gs_waTim"])
            vector.tensor_copy(out=scbRe, in_=gA[:, 0:DL * CH])
            vector.tensor_copy(out=scbIm, in_=gB[:, 0:DL * CH]).then_inc(VS, 1)

            for k in range(NB):
                vector.wait_ge(TS, M[f"ts_2ct{k}"])
                if k >= 2:
                    vector.wait_ge(SYP[k % 2], M[f"sy_{k-2}"])
                vector.tensor_copy(out=yt_sb[k % 2][:, :],
                                   in_=psT[k % 2][:, :]).then_inc(VS, 1)

        @block.scalar
        def _(scalar):
            sxc = [0]

            def sg(instr):
                sxc[0] += 1
                instr.then_inc(SX, 1)
                scalar.wait_ge(SX, sxc[0])
                return instr

            scalar.wait_ge(Q1, M["q1_params"])
            scalar.activation(out=d_dt[:, :], in_=p_ldt[:, :], func=AF.Exp)
            scalar.activation(out=d_elar[:, :], in_=p_lar[:, :],
                              func=AF.Exp).then_inc(SS, 1)
            for i in range(NT):
                scalar.wait_ge(VS, M[f"vs_rv{i}"])
                if i >= 2:
                    scalar.wait_ge(XW[i % 2], M[f"xw_{i-2}"])
                sg(scalar.activation(out=s_rstd[i % 2][:, :],
                                     in_=s_rstd[i % 2][:, :], func=AF.Sqrt))
                sg(scalar.activation(out=s_nmr[i % 2][:, :],
                                     in_=s_nmr[i % 2][:, :],
                                     func=AF.Identity,
                                     scale=s_rstd[i % 2][:, 0:1]))
                scalar.activation(out=xn_tl[i % 2], in_=x_tl[i % 3],
                                  func=AF.Identity, scale=s_rstd[i % 2][:, 0:1],
                                  bias=s_nmr[i % 2][:, 0:1]).then_inc(SS, 1)
                if i >= 1:
                    j = i - 1
                    scalar.wait_ge(VS, M[f"vs_clip{j}"])
                    if j >= 2:
                        scalar.wait_ge(SDX[j % 2], M[f"sdx_{j-2}"])
                    sg(scalar.activation(out=f8t[j % 2][:, :], in_=xq1_tl[j % 2],
                                         func=AF.Copy))
                    scalar.activation(out=xqbt[j % 2], in_=f8t[j % 2][:, :],
                                      func=AF.Copy, scale=2.0).then_inc(SS, 1)
                    sg(scalar.activation(out=s_dmy[:, :], in_=s_dmy[:, :],
                                         func=AF.Copy))
                    scalar.dma_start(out=xq_scr[j * 128:(j + 1) * 128, :],
                                     in_=xqbt[j % 2]).then_inc(SDX[j % 2], 16)
            j = NT - 1
            scalar.wait_ge(VS, M[f"vs_clip{j}"])
            sg(scalar.activation(out=f8t[j % 2][:, :], in_=xq1_tl[j % 2],
                                 func=AF.Copy))
            scalar.activation(out=xqbt[j % 2], in_=f8t[j % 2][:, :],
                              func=AF.Copy, scale=2.0).then_inc(SS, 1)
            sg(scalar.activation(out=s_dmy[:, :], in_=s_dmy[:, :], func=AF.Copy))
            scalar.dma_start(out=xq_scr[j * 128:(j + 1) * 128, :],
                             in_=xqbt[j % 2]).then_inc(SDX[j % 2], 16)
            gAv = gA[:, 0:GEN]
            gBv = gB[:, 0:GEN]
            gCv = gC[:, 0:GEN]
            gIi = gI[:, 0:GEN]
            scalar.wait_ge(VS, M["vs_k1"])
            scalar.activation(out=gIi, in_=gBv, func=AF.Copy)
            scalar.activation(out=gBv, in_=gIi, func=AF.Copy).then_inc(SS, 1)
            scalar.wait_ge(VS, M["vs_red1"])
            scalar.activation(out=gBv, in_=gBv, func=AF.Sin).then_inc(SS, 1)
            scalar.wait_ge(VS, M["vs_k2"])
            scalar.activation(out=gIi, in_=gAv, func=AF.Copy)
            scalar.activation(out=gAv, in_=gIi, func=AF.Copy).then_inc(SS, 1)
            scalar.wait_ge(VS, M["vs_red2"])
            scalar.activation(out=gCv, in_=gCv, func=AF.Sin).then_inc(SS, 1)
            scalar.wait_ge(VS, M["vs_decay"])
            scalar.activation(out=gAv, in_=gAv, func=AF.Exp).then_inc(SS, 1)
            scalar.wait_ge(VS, M["vs_powb"])
            for jh in range(2):
                srcp = bass.AP(powb, jh * 64 * 64, [[2 * GEN, 128], [1, 64 * 64]])
                dstp = bass.AP(eT1, jh * 64 * 128,
                               [[T * 128, 128], [128, 64], [1, 64]])
                scalar.dma_start(out=dstp, in_=srcp).then_inc(GS, 16)
            scalar.wait_ge(GS, M["gs_eT"])
            scalar.wait_ge(VS, M["vs_waIm"])
            for th in range(2):
                srcp = bass.AP(powb, th * 64 * 64, [[2 * GEN, 128], [1, 64 * 64]])
                dstp = bass.AP(waT, th * 64 * DL * 64,
                               [[64, 128], [DL * 64, 64], [1, 64]])
                scalar.dma_start(out=dstp, in_=srcp).then_inc(GS, 16)
            scalar.wait_ge(GS, M["gs_waTim"])
            scalar.wait_ge(VS, M["vs_tkb"])
            srcp = bass.AP(t_KB, 1, [[NE, 128], [1, 128]])
            scalar.dma_start(out=kpadR[:, 0:128], in_=srcp).then_inc(GS, 16)
            scalar.dma_start(out=kpadR[:, 128:256], in_=z128[:, :]).then_inc(GS, 16)
            scalar.wait_ge(GS, M["gs_kpad"])
            scalar.wait_ge(VS, M["vs_rtneg"])
            for bb4 in range(4):
                dstp = bass.AP(rTre, bb4 * DL, [[1, 128], [4 * DL, 32]])
                scalar.dma_start(out=dstp, in_=gC[:, 0:32],
                                 allow_slow_non_contiguous=True).then_inc(GS, 16)
            for bb4 in range(4):
                dstp = bass.AP(rTim, bb4 * DL, [[1, 128], [4 * DL, 32]])
                scalar.dma_start(out=dstp, in_=d_aimn[:, :],
                                 allow_slow_non_contiguous=True).then_inc(GS, 16)
            scalar.wait_ge(GS, M["gs_rT"])
            for i in range(NT):
                scalar.wait_ge(TS, M[f"ts_ffn{i}"])
                if i >= 2:
                    scalar.wait_ge(GWS[i % 2], M[f"gw_{i-2}"])
                pF = psF if i % 2 == 0 else psY
                scalar.activation(out=gtmp[i % 2][:, 0:512],
                                  in_=pF[0][:, :], func=AF.Silu,
                                  scale=v_s2[:, 0:1])
                scalar.activation(out=gtmp[i % 2][:, 512:1024],
                                  in_=pF[1][:, :], func=AF.Silu,
                                  scale=v_s2[:, 0:1]).then_inc(SS, 1)
                sg(scalar.activation(out=s_dmy[:, :], in_=s_dmy[:, :],
                                     func=AF.Copy))
                scalar.dma_start(out=g_scr[i * 128:(i + 1) * 128, :],
                                 in_=gtmp[i % 2]).then_inc(GWS[i % 2], 16)
            for g in range(NGRP):
                scalar.wait_ge(VS, M[f"vs_uev{4*g+3}"])
                if g >= 2:
                    scalar.wait_ge(SU[g % 2], M[f"su_{g-2}"])
                for comp, dstT in ((0, uReD), (32, uImD)):
                    srcp = u_sb_ap(g, comp * GW,
                                   [[GW, 32], [GC * CH, 4], [1, GC * CH]])
                    dstp = bass.AP(dstT, g * GC * CH,
                                   [[4 * DL * CH, 32], [DL * CH, 4],
                                    [1, GC * CH]])
                    scalar.dma_start(out=dstp, in_=srcp).then_inc(SU[g % 2], 16)
            for k in range(NB):
                scalar.wait_ge(TS, M[f"ts_2cm{k}"])
                scalar.activation(out=yg_sb[k % 2][:, :], in_=psY[k % 2][:, :],
                                  func=AF.Gelu).then_inc(SS, 1)
                if k >= 1:
                    j = k - 1
                    scalar.wait_ge(VS, M[f"vs_yt{j}"])
                    if j >= 2:
                        scalar.wait_ge(SYP[j % 2], M[f"sy_{j-2}"])
                    dst = bass.AP(y_scr, 4 * j * TOKS,
                                  [[128, U], [TOKS, 4], [1, 128]])
                    scalar.dma_start(out=dst,
                                     in_=yt_sb[j % 2][:, :]).then_inc(SYP[j % 2], 16)
            j = NB - 1
            scalar.wait_ge(VS, M[f"vs_yt{j}"])
            scalar.wait_ge(SYP[j % 2], M[f"sy_{j-2}"])
            dst = bass.AP(y_scr, 4 * j * TOKS, [[128, U], [TOKS, 4], [1, 128]])
            scalar.dma_start(out=dst, in_=yt_sb[j % 2][:, :]).then_inc(SYP[j % 2], 16)
            scalar.wait_ge(Q1, M["q1_sscrw"])
            for i in range(NT):
                scalar.wait_ge(TS, M[f"ts_p3_{i}"])
                if i >= 4:
                    scalar.wait_ge(GPC, M[f"gpc_comb{i-4}"])
                pF = psF if i % 2 == 0 else psY
                scalar.activation(out=s_sb[i % 4][:, 0:512], in_=pF[0][:, :],
                                  func=AF.Copy)
                scalar.activation(out=s_sb[i % 4][:, 512:1024], in_=pF[1][:, :],
                                  func=AF.Copy).then_inc(SS, 1)

        @block.tensor
        def _(tensor):
            tensor.wait_ge(Q1, M["q1_tr15"])
            tensor.wait_ge(GPD, M["gpd_wft"])
            tensor.wait_ge(VS, M["vs_bffn"])
            for i in range(NT):
                if i >= 2:
                    tensor.wait_ge(SS, M[f"ss_fev{i-2}"])
                pF = psF if i % 2 == 0 else psY
                for kk in range(8):
                    lhsT = xdqT[:, kk * TL + i * 128: kk * TL + (i + 1) * 128]
                    for h in range(2):
                        tensor.matmul(pF[h][:, :], lhsT,
                                      wft_b[:, kk * D + h * 512: kk * D + h * 512 + 512],
                                      start=(kk == 0), stop=False)
                last = None
                for h in range(2):
                    last = tensor.matmul(pF[h][:, :], c_ones1[:, :],
                                         bffn_sb[0:1, h * 512:h * 512 + 512],
                                         start=False, stop=True)
                last.then_inc(TS, 1)

            tensor.wait_ge(Q1, M["q1_xall"])
            tensor.wait_ge(SS, M[f"ss_fev{NT-1}"])
            for k in range(NB):
                g = k // 4
                tensor.wait_ge(QW[g % 2], M[f"qw_{g}"])
                if k >= 4:
                    tensor.wait_ge(VS, M[f"vs_uev{k-4}"])
                last = None
                for ci in range(4):
                    ch = (k % 4) * 4 + ci
                    lhsT = wa_b[g % 2][:, ch * 64:(ch + 1) * 64]
                    rhs = bass.AP(x_all, g * GC + ch, [[U * 128, 128], [128, U]])
                    last = tensor.matmul(psUB[k % 4][0:64, ci * 128:(ci + 1) * 128],
                                         lhsT, rhs, start=True, stop=True)
                last.then_inc(TS, 1)

            tensor.wait_ge(VS, M["vs_scan"])
            for k in range(NB):
                g = k // 4
                tensor.wait_ge(Q1, M[f"q1_hk{g}"])
                tensor.wait_ge(Q1, M[f"q1_e{g}"])
                tensor.wait_ge(Q1, M[f"q1_sm{g}"])
                if k >= 2:
                    tensor.wait_ge(SS, M[f"ss_gel{k-2}"])
                last = None
                for ci in range(4):
                    ch = (k % 4) * 4 + ci
                    lhsT = hk_b[g % 2][:, ch * 128:(ch + 1) * 128]
                    rhs = bass.AP(x_all, g * GC + ch, [[U * 128, 128], [128, U]])
                    tensor.matmul(psY[k % 2][:, ci * 128:(ci + 1) * 128], lhsT, rhs,
                                  start=True, stop=False)
                    lhsT = e_b[g % 2][0:64, ch * 128:(ch + 1) * 128]
                    rhs = sm_b[g % 2][0:64, ch * U:(ch + 1) * U]
                    last = tensor.matmul(psY[k % 2][:, ci * 128:(ci + 1) * 128],
                                         lhsT, rhs, start=False, stop=True)
                last.then_inc(TS, 1)
                tensor.wait_ge(SS, M[f"ss_gel{k}"])
                if k >= 2:
                    tensor.wait_ge(VS, M[f"vs_yt{k-2}"])
                last = None
                for ci in range(4):
                    last = tensor.transpose(psT[k % 2][:, ci * 128:(ci + 1) * 128],
                                            yg_sb[k % 2][:, ci * 128:(ci + 1) * 128],
                                            c_j[:, :])
                last.then_inc(TS, 1)

            tensor.wait_ge(GPC, M["gpc_cc2"])
            tensor.wait_ge(GPD, M["gpd_wt"])
            tensor.wait_ge(SS, M[f"ss_fev{NT-1}"])
            tensor.wait_ge(GPD, M["gpd_battn"])
            for i in range(NT):
                tensor.wait_ge(QP[i % 4], M[f"qp_{i}"])
                if i >= 2:
                    tensor.wait_ge(SS, M[f"ss_sev{i-2}"])
                pF = psF if i % 2 == 0 else psY
                for kk in range(8):
                    lhsT = sst_tl[i % 4][:, kk * 128:(kk + 1) * 128]
                    for h in range(2):
                        tensor.matmul(pF[h][:, :], lhsT,
                                      wt_b[:, kk * D + h * 512: kk * D + h * 512 + 512],
                                      start=(kk == 0), stop=False)
                last = None
                for h in range(2):
                    last = tensor.matmul(pF[h][:, :], c_ones1[:, :],
                                         battn_bb[0:1, h * 512:h * 512 + 512],
                                         start=False, stop=True)
                last.then_inc(TS, 1)

    ctx.close()
    return nc


def kernel(L0_attn_x, L0_attn_log_dt, L0_attn_log_A_real, L0_attn_A_imag,
           L0_attn_C_real, L0_attn_C_imag, L0_attn_W, L0_attn_b,
           L0_ffn_weight_fp8, L0_ffn_bias, L0_ffn_input_scale,
           L0_ffn_weight_scale):
    x = np.ascontiguousarray(np.asarray(L0_attn_x, dtype=np.float32))
    WTh = np.ascontiguousarray(np.asarray(L0_attn_W, dtype=np.float32).T)
    WFTh = np.ascontiguousarray(np.asarray(L0_ffn_weight_fp8, dtype=np.float32).T)
    battn = np.ascontiguousarray(np.asarray(L0_attn_b, np.float32).reshape(1, D))
    bffn = np.ascontiguousarray(np.asarray(L0_ffn_bias, np.float32).reshape(1, D))
    iscale = np.ascontiguousarray(np.broadcast_to(
        np.asarray(L0_ffn_input_scale, np.float32).reshape(1, 1), (DL, 1)))
    wscale = np.ascontiguousarray(np.broadcast_to(
        np.asarray(L0_ffn_weight_scale, np.float32).reshape(1, 1), (DL, 1)))
    iota = np.ascontiguousarray(np.broadcast_to(
        np.arange(128, -1, -1, dtype=np.float32), (DL, NE)))
    jm = np.eye(128, dtype=np.float32)[::-1].copy().astype(ml_dtypes.bfloat16)
    o1 = np.ones((1, 128), dtype=np.float32).astype(ml_dtypes.bfloat16)

    ldt_a = np.asarray(L0_attn_log_dt, np.float32)
    lar_a = np.asarray(L0_attn_log_A_real, np.float32)
    aim_a = np.asarray(L0_attn_A_imag, np.float32)
    cre_a = np.asarray(L0_attn_C_real, np.float32)
    cim_a = np.asarray(L0_attn_C_imag, np.float32)

    nc = build_program()
    in_maps = []
    for core in range(NCORE):
        b, h = core // 2, core % 2
        sl = slice(DL * core, DL * (core + 1))
        in_maps.append(dict(
            xs=np.ascontiguousarray(x[b, h * TL:(h + 1) * TL, :]),
            ldt=np.ascontiguousarray(ldt_a[sl].reshape(DL, 1)),
            lar=np.ascontiguousarray(lar_a[sl]),
            aim=np.ascontiguousarray(aim_a[sl]),
            cre=np.ascontiguousarray(cre_a[sl]),
            cim=np.ascontiguousarray(cim_a[sl]),
            WT=WTh, WFT=WFTh, battn_r=battn, bffn_r=bffn,
            isc=iscale, wsc=wscale, iotaE=iota, jmat=jm, ones1=o1,
        ))
    res = run_bass_kernel_spmd(nc, in_maps, core_ids=list(range(NCORE)))
    out = np.empty((BB, L, D), dtype=np.float32)
    for core in range(NCORE):
        b, h = core // 2, core % 2
        out[b, h * TL:(h + 1) * TL, :] = res.results[core]["out"]
    return out


if __name__ == "__main__":
    build_program()
    print("build ok")
